# version 21
# baseline (speedup 1.0000x reference)
import numpy as np
import concourse.bass as bass
import concourse.mybir as mybir
from concourse.bass_utils import run_bass_kernel_spmd

F32 = mybir.dt.float32
BF16 = mybir.dt.bfloat16
I32 = mybir.dt.int32
AF = mybir.ActivationFunctionType
ALU = mybir.AluOpType
AX = mybir.AxisListType

S, D, H, KD, DV, HID = 4096, 1024, 8, 128, 2048, 3072
NT = S // 128
NSC = S // 512
KC = D // 128
NG = 16
KSCALE = float(KD) ** -0.5

_DEBUG_TAPS = ()

DMA_SEMS = ("LDC", "LX0", "LX1", "LT", "LH0", "LH1", "LA0", "LA1", "LW0", "LW1",
            "LG1", "LG2", "LG3", "LG4", "LG5", "LG6", "FT0", "FT1", "LT2",
            "LTS", "LTV", "FT2", "FT3", "PZ0", "PZ1",
            "SQ0", "SQ1", "SP0", "SP1", "SF0", "SF1", "SG0", "SG1", "SY0", "SY1")


def _consts():
    lin = np.linspace(0.0, 1.0, KD // 2)
    angle = np.repeat(1.0 / 10000.0 ** lin, 2)
    idx = np.arange(S, dtype=np.float64)
    sin = np.sin(idx[:, None] * angle[None, :])
    cos = np.cos(idx[:, None] * angle[None, :])
    cosf = np.ascontiguousarray(cos.T).astype(np.float32)
    sf2 = np.ascontiguousarray(sin.T)
    sf2[0::2, :] *= -1.0
    sinf2 = sf2.astype(np.float32)
    rswap = np.zeros((128, 128), np.float32)
    for t in range(64):
        rswap[2 * t + 1, 2 * t] = 1.0
        rswap[2 * t, 2 * t + 1] = 1.0
    ident = np.eye(128, dtype=np.float32)
    dec = np.log1p(-2.0 ** (-5.0 - np.arange(H, dtype=np.float64)))
    p = np.arange(128, dtype=np.float64)
    a2t = np.zeros((128, 16), np.float32)
    for h in range(H):
        for sub in range(2):
            a2t[:, h * 2 + sub] = np.exp(dec[h] * (p + 128 * sub)).astype(np.float32)
    bt = np.exp(-dec[None, :] * p[:, None]).astype(np.float32)
    nrmt = np.zeros((128, 256), np.float64)
    for h in range(H):
        Dh = np.exp(dec[h])
        for I in range(32):
            i = I * 128 + p
            nrmt[:, h * 32 + I] = np.sqrt((1.0 - Dh ** (i + 1)) / (1.0 - Dh))
    nrmt = nrmt.astype(np.float32)
    jj = np.arange(128)
    tri2 = np.ones((128, 256), np.float32)
    tri2[:, 0:128] = (jj[None, :] >= jj[:, None]).astype(np.float32)
    return dict(cosf=cosf, sinf2=sinf2, rswap=rswap, ident=ident,
                a2t=a2t, bt=bt, nrmt=nrmt, tri2=tri2), dec


def _c32(dec, h, delta):
    return float(np.float32(np.exp(dec[h] * 128.0 * delta) * 2.0 ** -3.5))


class Prog:
    def __init__(self, nc):
        self.nc = nc
        self.C = {k: 0 for k in DMA_SEMS + ("PE", "ACT", "DVE")}
        self.sems = {}
        self.marks = {}
        self.dry = False


class Eng:
    def __init__(self, p, name, eng=None):
        self.p = p
        self.name = name
        self.e = eng

    def op(self, fn, inc=None, mark=None):
        p = self.p
        n = None
        if inc is not None:
            step = 16 if inc in DMA_SEMS else 1
            p.C[inc] += step
            n = p.C[inc]
            if mark is not None:
                for mk in ([mark] if isinstance(mark, str) else mark):
                    if (not p.dry) and mk in p.marks and p.marks[mk] != (inc, n):
                        raise RuntimeError(f"mark drift {mk}: {p.marks[mk]} vs {(inc, n)}")
                    p.marks[mk] = (inc, n)
        if not p.dry:
            ins = fn(self.e)
            if inc is not None:
                ins.then_inc(p.sems[inc], 16 if inc in DMA_SEMS else 1)
        return n

    def wait(self, mark=None, sem=None, val=None):
        p = self.p
        if p.dry:
            return
        if mark is not None:
            sem, val = p.marks[mark]
        self.e.wait_ge(p.sems[sem], val)

    def wait_if(self, mark):
        if self.p.dry:
            return
        if mark in self.p.marks:
            self.wait(mark=mark)


def _run_stage(p, name, bodies):
    snap = dict(p.C)
    p.dry = True
    for en, fn in bodies:
        fn(Eng(p, en))
    p.C.clear()
    p.C.update(snap)
    p.dry = False
    with p.nc.Block(name) as blk:
        for en, fn in bodies:
            getattr(blk, en)(lambda eng, fn=fn, en=en: fn(Eng(p, en, eng)))


def _mm_group(w, out_ap, pairs, mark=None):
    n = len(pairs)
    for i, (l, r) in enumerate(pairs):
        last = i == n - 1
        w.op(lambda e, l=l, r=r, i=i, last=last:
             e.matmul(out_ap, l, r, start=(i == 0), stop=last),
             inc="PE" if last else None, mark=mark if last else None)


def _build(nstages=7):
    nc = bass.Bass("TRN2", target_bir_lowering=False)
    _, dec = _consts()

    def din(name, shape, dt):
        return nc.dram_tensor(name, shape, dt, kind="ExternalInput")

    x_h = din("x", [S, D], F32)
    wq_h = din("wq", [D, D], F32)
    wk_h = din("wk", [D, D], F32)
    wv_h = din("wv", [D, DV], F32)
    wga_h = din("wga", [D, DV], F32)
    wo_h = din("wo", [DV, D], F32)
    fg_h = din("fg", [HID, D], I32)
    fu_h = din("fu", [HID, D], I32)
    fd_h = din("fd", [D, HID], I32)
    cosf_h = din("cosf", [128, S], F32)
    sinf2_h = din("sinf2", [128, S], F32)
    rswap_h = din("rswap", [128, 128], F32)
    ident_h = din("ident", [128, 128], F32)
    a2t_h = din("a2t", [128, 16], F32)
    bt_h = din("bt", [128, 8], F32)
    nrmt_h = din("nrmt", [128, 256], F32)
    tri2_h = din("tri2", [128, 256], F32)
    y_h = nc.dram_tensor("y", [S, D], F32, kind="ExternalOutput")

    def scr(name, shape, dt):
        kind = "ExternalOutput" if name in _DEBUG_TAPS else "Internal"
        return nc.dram_tensor(name, shape, dt, kind=kind)

    qs_d = scr("qs_d", [H, 128, S], BF16)
    ks_d = scr("ks_d", [H, 128, S], BF16)
    v_d = scr("v_d", [H, S, 256], BF16)
    g_d = scr("g_d", [H, S, 256], BF16)
    fgb_d = scr("fgb_d", [HID, D], BF16)
    fub_d = scr("fub_d", [HID, D], BF16)
    fdb_d = scr("fdb_d", [D, HID], BF16)
    hid_d = scr("hid_d", [NSC, 24, 128, 512], BF16)
    gfin_d = scr("gfin_d", [S, D], F32)
    gated_d = scr("gated_d", [H, 2, S, 128], BF16)
    xn_d = scr("xn_d", [S, D], BF16) if "xn_d" in _DEBUG_TAPS else None

    a2_sb = nc.alloc_sbuf_tensor("a2_sb", [128, 16], F32)
    bt_sb = nc.alloc_sbuf_tensor("bt_sb", [128, 8], F32)
    nrm_sb = nc.alloc_sbuf_tensor("nrm_sb", [128, 256], F32)
    tri2_sb = nc.alloc_sbuf_tensor("tri2_sb", [128, 256], F32)
    rm_sb = nc.alloc_sbuf_tensor("rm_sb", [128, 128], BF16)
    id_sb = nc.alloc_sbuf_tensor("id_sb", [128, 128], BF16)
    ones_sb = nc.alloc_sbuf_tensor("ones_sb", [128, 32], BF16)
    eps_sb = nc.alloc_sbuf_tensor("eps_sb", [128, 1], F32)
    sm_sb = nc.alloc_sbuf_tensor("sm_sb", [128, 32], F32)
    sq_sb = nc.alloc_sbuf_tensor("sq_sb", [128, 32], F32)
    m_sb = nc.alloc_sbuf_tensor("m_sb", [128, 32], F32)
    var_sb = nc.alloc_sbuf_tensor("var_sb", [128, 32], F32)
    rstd_sb = nc.alloc_sbuf_tensor("rstd_sb", [128, 32], F32)
    nb_sb = nc.alloc_sbuf_tensor("nb_sb", [128, 32], F32)
    t0e_sb = nc.alloc_sbuf_tensor("t0e_sb", [128, 6], F32)
    ssqe_sb = nc.alloc_sbuf_tensor("ssqe_sb", [128, 6], F32)
    bete_sb = nc.alloc_sbuf_tensor("bete_sb", [128, 6], F32)
    inv0e_sb = nc.alloc_sbuf_tensor("inv0e_sb", [128, 6], F32)
    inve_sb = nc.alloc_sbuf_tensor("inve_sb", [128, 6], F32)
    inv2e_sb = nc.alloc_sbuf_tensor("inv2e_sb", [128, 6], F32)
    m2e_sb = nc.alloc_sbuf_tensor("m2e_sb", [128, 6], F32)
    le_sb = nc.alloc_sbuf_tensor("le_sb", [128, 6], F32)
    rse_sb = nc.alloc_sbuf_tensor("rse_sb", [128, 6], F32)
    sige_sb = nc.alloc_sbuf_tensor("sige_sb", [128, 6], F32)
    junk_sb = nc.alloc_sbuf_tensor("junk_sb", [128, 256], BF16)

    ob = nc.lookup_mloc(junk_sb).addr + 512
    ob = (ob + 63) & ~63
    TOP = 229344

    def at(name, shape, dt, off):
        h = nc.alloc_sbuf_tensor_at(name, shape, dt, offset=ob + off)
        per = int(np.prod(shape[1:])) * mybir.dt.size(dt)
        assert ob + off + per <= TOP, (name, ob + off + per)
        return h

    KB = 1024
    xn_f = at("xn_f", [128, KC, S], BF16, 0)
    xln_sb = at("xln_sb", [128, 2, D], F32, 64 * KB)
    xnt_sb = at("xnt_sb", [128, 2, D], BF16, 72 * KB)
    lnjunk_sb = at("lnjunk_sb", [128, D], F32, 76 * KB)
    cos_sb = at("cos_sb", [128, S], F32, 80 * KB)
    sin_sb = at("sin_sb", [128, S], F32, 96 * KB)
    wqk_sb = at("wqk_sb", [128, 2, KC, D], BF16, 112 * KB)
    qsa_sb = at("qsa_sb", [128, 2, S], BF16, 144 * KB)
    ksa_sb = at("ksa_sb", [128, 2, S], BF16, 160 * KB)
    qt_sb = at("qt_sb", [128, 2, 512], BF16, 176 * KB)
    t1_sb = at("t1_sb", [128, 2, 512], F32, 178 * KB)
    t2_sb = at("t2_sb", [128, 2, 512], F32, 186 * KB)
    wvg_n = at("wvg_n", [128, 4, 2, KC, 512], BF16, 64 * KB)
    vout_sb = at("vout_sb", [128, 2, 512], BF16, 128 * KB)
    gout_sb = at("gout_sb", [128, 2, 512], BF16, 130 * KB)
    wgt_sb = at("wgt_sb", [128, KC, HID], BF16, 64 * KB)
    wut_sb = at("wut_sb", [128, KC, HID], BF16, 112 * KB)
    sg_sb = at("sg_sb", [128, 2, 512], F32, 196 * KB)
    hidt_sb = at("hidt_sb", [128, 2, 512], BF16, 200 * KB)
    hid_sb = at("hid_sb", [128, 2, 24, 512], BF16, 49 * KB)
    gfin_sb = at("gfin_sb", [128, 2, 512], F32, 97 * KB)
    wdta_sb = at("wdta_sb", [128, 18, D], BF16, 160 * KB)
    wdtb_sb = at("wdtb_sb", [128, 6, D], BF16, 101 * KB)
    vA_sb = at("vA_sb", [128, 32, 257], BF16, 0)
    qsA_sb = at("qsA_sb", [128, S], BF16, 17 * KB)
    ksA_sb = at("ksA_sb", [128, S], BF16, 25 * KB)
    gA_sb = at("gA_sb", [128, 32, 256], BF16, 33 * KB)
    vB_sb = at("vB_sb", [128, 32, 257], BF16, 64 * KB)
    qsB_sb = at("qsB_sb", [128, S], BF16, 81 * KB)
    ksB_sb = at("ksB_sb", [128, S], BF16, 89 * KB)
    gB_sb = at("gB_sb", [128, 32, 256], BF16, 97 * KB)
    gated_sb = at("gated_sb", [128, 2, 32, 256], BF16, 113 * KB)
    sbfA_sb = at("sbfA_sb", [128, 4, 256], BF16, 145 * KB)
    sbfB_sb = at("sbfB_sb", [128, 4, 256], BF16, 49 * KB)
    gT_sb = at("gT_sb", [128, 16, S], BF16, 0)
    wo_sb = at("wo_sb", [128, 16, D], BF16, 147 * KB)
    gf_sb = at("gf_sb", [128, 2, D], F32, 179 * KB)
    xw_sb = at("xw_sb", [128, 2, D], F32, 187 * KB)
    outw_sb = at("outw_sb", [128, 2, D], F32, 195 * KB)

    vAB = [vA_sb, vB_sb]
    qsAB = [qsA_sb, qsB_sb]
    ksAB = [ksA_sb, ksB_sb]
    gAB = [gA_sb, gB_sb]

    def _sbf(gsi):
        return (sbfA_sb if gsi % 8 < 4 else sbfB_sb)[:, gsi % 4]

    p = Prog(nc)
    import contextlib
    stack = contextlib.ExitStack()
    for s in DMA_SEMS + ("PE", "ACT", "DVE"):
        p.sems[s] = stack.enter_context(nc.semaphore(s))

    jlists = {}
    for h in range(H):
        for G in range(NG):
            jlists[(h, G)] = [J for J in range(2 * G + 2)
                             if _c32(dec, h, 2 * G - J) != 0.0]

    def ln_sync(w):
        w.op(lambda e: e.dma_start(cos_sb[:, :], cosf_h[:, :]), inc="LDC")
        w.op(lambda e: e.dma_start(sin_sb[:, :], sinf2_h[:, :]), inc="LDC")
        w.op(lambda e: e.dma_start(a2_sb[:, :], a2t_h[:, :]), inc="LDC")
        w.op(lambda e: e.dma_start(bt_sb[:, :], bt_h[:, :]), inc="LDC")
        w.op(lambda e: e.dma_start(nrm_sb[:, :], nrmt_h[:, :]), inc="LDC")
        w.op(lambda e: e.dma_start(tri2_sb[:, :], tri2_h[:, :]), inc="LDC", mark="c_all")
        for t in range(NT):
            if t >= 2:
                w.wait(mark=f"ln_sq_{t-2}")
                w.wait(mark=f"ln_sm_{t-2}")
            w.op(lambda e, t=t: e.dma_start(xln_sb[:, t % 2, :], x_h[t * 128:(t + 1) * 128, :]),
                 inc=f"LX{t % 2}", mark=f"ln_x_{t}")
        for t in range(NT):
            if t == 0:
                w.wait(mark="ln_nb")
                w.wait(mark="ln_exp")
            if t >= 2:
                w.wait(mark=f"ln_ap_{t-2}")
            w.op(lambda e, t=t: e.dma_start(xln_sb[:, t % 2, :], x_h[t * 128:(t + 1) * 128, :]),
                 inc=f"LX{t % 2}", mark=f"ln_x2_{t}")

    def ln_gpsimd(w):
        w.op(lambda e: e.dma_start(rm_sb[:, :], rswap_h[:, :]), inc="LG1")
        w.op(lambda e: e.dma_start(id_sb[:, :], ident_h[:, :]), inc="LG1")
        w.op(lambda e: e.dma_start(wqk_sb[:, 0, :, :],
                                   wq_h.ap().rearrange("(kk p) d -> p kk d", p=128)),
             inc="LG1")
        w.op(lambda e: e.dma_start(wqk_sb[:, 1, :, :],
                                   wk_h.ap().rearrange("(kk p) d -> p kk d", p=128)),
             inc="LG1", mark="w_qk_all")

    def ln_vector(w):
        w.op(lambda e: e.memset(eps_sb[:, :], 1e-5), inc="DVE")
        w.op(lambda e: e.memset(ones_sb[:, :], 1.0), inc="DVE", mark="v_ones")
        for t in range(NT):
            w.wait(mark=f"ln_x_{t}")
            w.op(lambda e, t=t: e.reduce_sum(sm_sb[:, t:t + 1], xln_sb[:, t % 2, :], axis=AX.X),
                 inc="DVE", mark=f"ln_sm_{t}")
        w.wait(mark="ln_sq_31")
        w.wait(mark="ln_sm_31")
        n = w.op(lambda e: e.tensor_scalar_mul(m_sb[:, :], sm_sb[:, :], 1.0 / D), inc="DVE")
        w.wait(sem="DVE", val=n)
        n = w.op(lambda e: e.tensor_scalar_mul(var_sb[:, :], sq_sb[:, :], 1.0 / D), inc="DVE")
        w.wait(sem="DVE", val=n)
        n = w.op(lambda e: e.scalar_tensor_tensor(rstd_sb[:, :], m_sb[:, :], -1.0, m_sb[:, :],
                                                  ALU.mult, ALU.mult), inc="DVE")
        w.wait(sem="DVE", val=n)
        n = w.op(lambda e: e.tensor_add(var_sb[:, :], var_sb[:, :], rstd_sb[:, :]), inc="DVE")
        w.wait(sem="DVE", val=n)
        n = w.op(lambda e: e.tensor_scalar_max(var_sb[:, :], var_sb[:, :], 0.0),
                 inc="DVE", mark="ln_var")
        w.wait(sem="DVE", val=n)
        w.wait(mark="ln_exp")
        w.op(lambda e: e.scalar_tensor_tensor(nb_sb[:, :], m_sb[:, :], -1.0, rstd_sb[:, :],
                                              ALU.mult, ALU.mult), inc="DVE", mark="ln_nb")
        ps = p.psums
        for t in range(NT):
            for k in range(KC):
                idx = t * KC + k
                w.wait(mark=f"ln_tr_{idx}")
                w.op(lambda e, t=t, k=k, idx=idx:
                     e.tensor_copy(xn_f[:, k, t * 128:(t + 1) * 128], ps[idx % 2][:, 0:128]),
                     inc="DVE", mark=f"ln_xf_{idx}")

    def ln_scalar(w):
        prev = [None]
        for t in range(NT):
            w.wait(mark=f"ln_x_{t}")
            if prev[0] is not None:
                w.wait(sem="ACT", val=prev[0])
            prev[0] = w.op(lambda e, t=t: e.activation(lnjunk_sb[:, :], xln_sb[:, t % 2, :],
                                                       AF.Square, accum_out=sq_sb[:, t:t + 1]),
                           inc="ACT", mark=f"ln_sq_{t}")
        w.wait(mark="ln_var")
        n = w.op(lambda e: e.activation(rstd_sb[:, :], var_sb[:, :], AF.Ln, bias=eps_sb[:, 0:1]),
                 inc="ACT")
        w.wait(sem="ACT", val=n)
        w.op(lambda e: e.activation(rstd_sb[:, :], rstd_sb[:, :], AF.Exp, scale=-0.5),
             inc="ACT", mark="ln_exp")
        w.wait(mark="ln_nb")
        w.wait(mark="ln_exp")
        for t in range(NT):
            w.wait(mark=f"ln_x2_{t}")
            if t >= 2:
                w.wait(mark=f"ln_tr_{(t - 2) * KC + KC - 1}")
            w.op(lambda e, t=t: e.activation(xnt_sb[:, t % 2, :], xln_sb[:, t % 2, :], AF.Identity,
                                             scale=rstd_sb[:, t:t + 1], bias=nb_sb[:, t:t + 1]),
                 inc="ACT", mark=f"ln_ap_{t}")

    def ln_tensor(w):
        ps = p.psums
        w.wait(mark="w_qk_all")
        for t in range(NT):
            w.wait(mark=f"ln_ap_{t}")
            for k in range(KC):
                idx = t * KC + k
                if idx >= 2:
                    w.wait(mark=f"ln_xf_{idx-2}")
                w.op(lambda e, t=t, k=k, idx=idx:
                     e.transpose(ps[idx % 2][:, 0:128], xnt_sb[:, t % 2, k * 128:(k + 1) * 128],
                                 id_sb[:, :]),
                     inc="PE", mark=f"ln_tr_{idx}")

    def pqk_tensor(w):
        w.wait(mark="w_qk_all")
        for ti, (wsel, _asm, pref) in enumerate(((0, qsa_sb, "q"), (1, ksa_sb, "k"))):
            for h in range(H):
                for sc in range(NSC):
                    gi = ti * 64 + h * NSC + sc
                    if gi >= 2:
                        w.wait(mark=f"pqk_ev_{gi-2}")
                    _mm_group(w, p.psums[gi % 2][:, 0:512],
                              [(wqk_sb[:, wsel, kk, h * 128:(h + 1) * 128],
                                xn_f[:, kk, sc * 512:(sc + 1) * 512]) for kk in range(KC)],
                              mark=f"pqk_mm_{gi}")
                    if gi >= 1:
                        _pqk_rot(w, gi - 1)
        _pqk_rot(w, 127)

    def _pqk_rot(w, gi):
        if gi >= 2:
            w.wait(mark=f"pqk_t2_{gi-2}")
        w.wait(mark=f"pqk_ev_{gi}")
        w.op(lambda e, gi=gi: e.matmul(p.psums[2 + gi % 2][:, 0:512], rm_sb[:, :],
                                       qt_sb[:, gi % 2, :], start=True, stop=True),
             inc="PE", mark=f"pqk_rot_{gi}")

    def pqk_scalar(w):
        for ti in range(2):
            for h in range(H):
                for sc in range(NSC):
                    gi = ti * 64 + h * NSC + sc
                    w.wait(mark=f"pqk_mm_{gi}")
                    if gi >= 2:
                        w.wait(mark=f"pqk_rot_{gi-2}")
                        w.wait(mark=f"pqk_t1_{gi-2}")
                    w.op(lambda e, gi=gi: e.activation(qt_sb[:, gi % 2, :],
                                                       p.psums[gi % 2][:, 0:512], AF.Copy),
                         inc="ACT", mark=f"pqk_ev_{gi}")

    def pqk_sync(w):
        for ti in range(2):
            asm = qsa_sb if ti == 0 else ksa_sb
            dst = qs_d if ti == 0 else ks_d
            for h in range(H):
                w.wait(mark=f"pqk_add_{ti}_{h}_{NSC-1}")
                w.op(lambda e, h=h, asm=asm, dst=dst:
                     e.dma_start(dst[h, :, :], asm[:, h % 2, :]),
                     inc=f"SQ{h % 2}", mark=f"pqk_st_{ti}_{h}")

    def pqk_vector(w):
        w.wait(mark="c_all")

        def emit_add(gi):
            ti, r = divmod(gi, 64)
            h, sc = divmod(r, NSC)
            asm = qsa_sb if ti == 0 else ksa_sb
            if h >= 2 and sc == 0:
                w.wait(mark=f"pqk_st_{ti}_{h-2}")
            w.wait(mark=f"pqk_t1_{gi}")
            w.wait(mark=f"pqk_t2_{gi}")
            w.op(lambda e, gi=gi, sc=sc, h=h, asm=asm:
                 e.tensor_add(asm[:, h % 2, sc * 512:(sc + 1) * 512],
                              t1_sb[:, gi % 2, :], t2_sb[:, gi % 2, :]),
                 inc="DVE", mark=f"pqk_add_{ti}_{h}_{sc}")

        for ti in range(2):
            for h in range(H):
                for sc in range(NSC):
                    gi = ti * 64 + h * NSC + sc
                    w.wait(mark=f"pqk_ev_{gi}")
                    if gi >= 2:
                        ti2, r2 = divmod(gi - 2, 64)
                        h2, sc2 = divmod(r2, NSC)
                        w.wait(mark=f"pqk_add_{ti2}_{h2}_{sc2}")
                    w.op(lambda e, gi=gi, sc=sc:
                         e.tensor_mul(t1_sb[:, gi % 2, :], qt_sb[:, gi % 2, :],
                                      cos_sb[:, sc * 512:(sc + 1) * 512]),
                         inc="DVE", mark=f"pqk_t1_{gi}")
                    w.wait(mark=f"pqk_rot_{gi}")
                    w.op(lambda e, gi=gi, sc=sc:
                         e.tensor_mul(t2_sb[:, gi % 2, :], p.psums[2 + gi % 2][:, 0:512],
                                      sin_sb[:, sc * 512:(sc + 1) * 512]),
                         inc="DVE", mark=f"pqk_t2_{gi}")
                    if gi >= 1:
                        emit_add(gi - 1)
        emit_add(127)

    def pvg_gpsimd(w):
        w.op(lambda e: e.dma_start(fgb_d[:, :], fg_h[:, :]), inc="LG2")
        w.op(lambda e: e.dma_start(fub_d[:, :], fu_h[:, :]), inc="LG2")
        nwaits = {0: ("ln_ap_31", "ln_tr_255", "ln_sq_31"),
                  1: ("pqk_t1_127",), 2: ("pqk_t2_127",), 3: ("pqk_mm_127",)}
        for n in range(4):
            for mk in nwaits[n]:
                w.wait(mark=mk)
            nsem = ("LG5", "LG6", "PZ0", "PZ1")[n]
            w.op(lambda e, n=n, nsem=nsem: e.dma_start(
                     wvg_n[:, n, 0, :, :],
                     wv_h[:, n * 512:(n + 1) * 512].rearrange("(kk p) f -> p kk f", p=128)),
                 inc=nsem)
            w.op(lambda e, n=n, nsem=nsem: e.dma_start(
                     wvg_n[:, n, 1, :, :],
                     wga_h[:, n * 512:(n + 1) * 512].rearrange("(kk p) f -> p kk f", p=128)),
                 inc=nsem, mark=f"w_vga_{n}")
        w.op(lambda e: e.dma_start(fdb_d[:, :], fd_h[:, :]), inc="LG2",
             mark="w_f_all")

    def pvg_tensor(w):
        for n in range(4):
            w.wait(mark=f"w_vga_{n}")
            for t in range(NT):
                vi = n * NT + t
                if vi >= 2:
                    w.wait(mark=f"pvg_vev_{vi-2}")
                _mm_group(w, p.psums[vi % 2][:, 0:512],
                          [(xn_f[:, kk, t * 128:(t + 1) * 128],
                            wvg_n[:, n, 0, kk, :]) for kk in range(KC)],
                          mark=f"pvg_vmm_{vi}")
                if vi >= 2:
                    w.wait(mark=f"pvg_gev_{vi-2}")
                _mm_group(w, p.psums[2 + vi % 2][:, 0:512],
                          [(xn_f[:, kk, t * 128:(t + 1) * 128],
                            wvg_n[:, n, 1, kk, :]) for kk in range(KC)],
                          mark=f"pvg_gmm_{vi}")

    def pvg_vector(w):
        w.wait(mark="c_all")
        for n in range(4):
            for t in range(NT):
                vi = n * NT + t
                w.wait(mark=f"pvg_vmm_{vi}")
                if vi >= 2:
                    w.wait(mark=f"pvg_s_{vi-2}")
                for half in range(2):
                    w.op(lambda e, vi=vi, n=n, half=half:
                         e.tensor_scalar_mul(vout_sb[:, vi % 2, half * 256:(half + 1) * 256],
                                             p.psums[vi % 2][:, half * 256:(half + 1) * 256],
                                             bt_sb[:, 2 * n + half:2 * n + half + 1]),
                         inc="DVE", mark=f"pvg_vev_{vi}" if half == 1 else None)

    def pvg_scalar(w):
        for n in range(4):
            for t in range(NT):
                vi = n * NT + t
                w.wait(mark=f"pvg_gmm_{vi}")
                if vi >= 2:
                    w.wait(mark=f"pvg_s_{vi-2}")
                w.op(lambda e, vi=vi: e.activation(gout_sb[:, vi % 2, :],
                                                   p.psums[2 + vi % 2][:, 0:512], AF.Silu),
                     inc="ACT", mark=f"pvg_gev_{vi}")
                w.wait(mark=f"pvg_vev_{vi}")
                w.wait(mark=f"pvg_gev_{vi}")
                for half in range(2):
                    w.op(lambda e, vi=vi, t=t, n=n, half=half:
                         e.dma_start(v_d[2 * n + half, t * 128:(t + 1) * 128, :],
                                     vout_sb[:, vi % 2, half * 256:(half + 1) * 256]),
                         inc=f"SP{vi % 2}")
                    w.op(lambda e, vi=vi, t=t, n=n, half=half:
                         e.dma_start(g_d[2 * n + half, t * 128:(t + 1) * 128, :],
                                     gout_sb[:, vi % 2, half * 256:(half + 1) * 256]),
                         inc=f"SP{vi % 2}", mark=f"pvg_s_{vi}" if half == 1 else None)

    def fa_sync(w):
        w.wait(mark="w_f_all")
        w.wait(mark="pqk_st_1_6")
        w.wait(mark="pqk_st_1_7")
        w.wait(mark="pvg_s_126")
        w.wait(mark="pvg_s_127")
        for g in range(6):
            if g >= 2:
                w.wait(mark=f"fa_umm_{(4 * (g - 2) + 3) * NSC + NSC - 1}")
            for kk in range(KC):
                w.op(lambda e, g=g, kk=kk: e.dma_start_transpose(
                         wgt_sb[:, kk, g * 512:(g + 1) * 512],
                         fgb_d[g * 512:(g + 1) * 512, kk * 128:(kk + 1) * 128]),
                     inc=f"FT{(g % 2) * 2}", mark=f"fa_wg_{g}" if kk == KC - 1 else None)
            for kk in range(KC):
                w.op(lambda e, g=g, kk=kk: e.dma_start_transpose(
                         wut_sb[:, kk, g * 512:(g + 1) * 512],
                         fub_d[g * 512:(g + 1) * 512, kk * 128:(kk + 1) * 128]),
                     inc=f"FT{(g % 2) * 2 + 1}", mark=f"fa_wu_{g}" if kk == KC - 1 else None)
        for hc in range(18):
            w.op(lambda e, hc=hc: e.dma_start_transpose(wdta_sb[:, hc, :],
                                                        fdb_d[:, hc * 128:(hc + 1) * 128]),
                 inc="LT")

    def fa_tensor(w):
        for hc in range(24):
            if hc % 4 == 0:
                w.wait(mark=f"fa_wg_{hc // 4}")
                w.wait(mark=f"fa_wu_{hc // 4}")
            for sc in range(NSC):
                gi = hc * NSC + sc
                if gi >= 2:
                    w.wait(mark=f"fa_sg_{gi-2}")
                _mm_group(w, p.psums[gi % 2][:, 0:512],
                          [(wgt_sb[:, kk, hc * 128:(hc + 1) * 128],
                            xn_f[:, kk, sc * 512:(sc + 1) * 512]) for kk in range(KC)],
                          mark=f"fa_gmm_{gi}")
                if gi >= 2:
                    w.wait(mark=f"fa_hid_{gi-2}")
                _mm_group(w, p.psums[2 + gi % 2][:, 0:512],
                          [(wut_sb[:, kk, hc * 128:(hc + 1) * 128],
                            xn_f[:, kk, sc * 512:(sc + 1) * 512]) for kk in range(KC)],
                          mark=f"fa_umm_{gi}")

    def fa_scalar(w):
        for hc in range(24):
            for sc in range(NSC):
                gi = hc * NSC + sc
                w.wait(mark=f"fa_gmm_{gi}")
                if gi >= 2:
                    w.wait(mark=f"fa_hid_{gi-2}")
                w.op(lambda e, gi=gi: e.activation(sg_sb[:, gi % 2, :],
                                                   p.psums[gi % 2][:, 0:512],
                                                   AF.Silu, scale=1.0 / 127.0),
                     inc="ACT", mark=f"fa_sg_{gi}")
                w.wait(mark=f"fa_hid_{gi}")
                w.op(lambda e, gi=gi, hc=hc, sc=sc:
                     e.dma_start(hid_d[sc, hc, :, :], hidt_sb[:, gi % 2, :]),
                     inc=f"SF{gi % 2}", mark=f"fa_st_{gi}")

    def fa_vector(w):
        for hc in range(24):
            for sc in range(NSC):
                gi = hc * NSC + sc
                w.wait(mark=f"fa_umm_{gi}")
                w.wait(mark=f"fa_sg_{gi}")
                if gi >= 2:
                    w.wait(mark=f"fa_st_{gi-2}")
                w.op(lambda e, gi=gi: e.scalar_tensor_tensor(hidt_sb[:, gi % 2, :],
                                                             p.psums[2 + gi % 2][:, 0:512],
                                                             1.0 / 127.0, sg_sb[:, gi % 2, :],
                                                             ALU.mult, ALU.mult),
                     inc="DVE", mark=f"fa_hid_{gi}")

    def fb_sync(w):
        for hc in range(18, 24):
            w.op(lambda e, hc=hc: e.dma_start_transpose(wdtb_sb[:, hc - 18, :],
                                                        fdb_d[:, hc * 128:(hc + 1) * 128]),
                 inc="LT", mark="fb_w" if hc == 23 else None)
        w.wait(mark=f"fa_st_{24 * NSC - 2}")
        w.wait(mark=f"fa_st_{24 * NSC - 1}")
        for sc in range(NSC):
            if sc >= 2:
                w.wait(mark=f"fb_mm_{(sc - 2) * 8 + 7}")
            w.op(lambda e, sc=sc: e.dma_start(hid_sb[:, sc % 2, :, :],
                                              hid_d[sc].rearrange("hc p f -> p hc f")),
                 inc=f"LH{sc % 2}", mark=f"fb_hid_{sc}")
        w.wait(mark="pqk_st_1_6")
        w.wait(mark="pqk_st_1_7")
        w.wait(mark="pvg_s_126")
        w.wait(mark="pvg_s_127")
        w.op(lambda e: e.dma_start(qsA_sb[:, :], qs_d[0, :, :]), inc="LA0")
        w.op(lambda e: e.dma_start(ksA_sb[:, :], ks_d[0, :, :]), inc="LA0")
        w.op(lambda e: e.dma_start(vA_sb[:, :, 0:256],
                                   v_d[0].rearrange("(t p) f -> p t f", p=128)),
             inc="LA0")
        w.op(lambda e: e.dma_start(gA_sb[:, :, :],
                                   g_d[0].rearrange("(t p) f -> p t f", p=128)),
             inc="LA0", mark="ag_0")

    def fb_tensor(w):
        w.wait(mark="fb_w")
        for sc in range(NSC):
            w.wait(mark=f"fb_hid_{sc}")
            for ss in range(4):
                for n2 in range(2):
                    di = sc * 8 + ss * 2 + n2
                    if di >= 2:
                        w.wait(mark=f"fb_gf_{di-2}")
                    _mm_group(w, p.psums[di % 2][:, 0:512],
                              [(hid_sb[:, sc % 2, hc, ss * 128:(ss + 1) * 128],
                                (wdta_sb[:, hc, n2 * 512:(n2 + 1) * 512] if hc < 18 else
                                 wdtb_sb[:, hc - 18, n2 * 512:(n2 + 1) * 512]))
                               for hc in range(24)],
                              mark=f"fb_mm_{di}")

    def fb_scalar(w):
        for sc in range(NSC):
            for ss in range(4):
                for n2 in range(2):
                    di = sc * 8 + ss * 2 + n2
                    w.wait(mark=f"fb_mm_{di}")
                    if di >= 2:
                        w.wait(mark=f"fb_st_{di-2}")
                    w.op(lambda e, di=di: e.activation(gfin_sb[:, di % 2, :],
                                                       p.psums[di % 2][:, 0:512],
                                                       AF.Silu, scale=1.0 / 127.0),
                         inc="ACT", mark=f"fb_gf_{di}")
                    r0 = sc * 512 + ss * 128
                    w.wait(sem="ACT", val=p.C["ACT"])
                    w.op(lambda e, di=di, r0=r0, n2=n2:
                         e.dma_start(gfin_d[r0:r0 + 128, n2 * 512:(n2 + 1) * 512],
                                     gfin_sb[:, di % 2, :]),
                         inc=f"SF{di % 2}", mark=f"fb_st_{di}")

    def att_sync(w):
        w.wait(mark="fb_st_62")
        w.wait(mark="fb_st_63")
        for h in range(1, H):
            if h >= 2:
                w.wait(mark=f"ahead_{h-2}")
                w.wait(mark=f"agate_{h-2}_15_1")
            buf = h % 2
            w.op(lambda e, h=h, buf=buf: e.dma_start(qsAB[buf][:, :], qs_d[h, :, :]),
                 inc=f"LA{h % 2}")
            w.op(lambda e, h=h, buf=buf: e.dma_start(ksAB[buf][:, :], ks_d[h, :, :]),
                 inc=f"LA{h % 2}")
            w.op(lambda e, h=h, buf=buf: e.dma_start(vAB[buf][:, :, 0:256],
                                                     v_d[h].rearrange("(t p) f -> p t f", p=128)),
                 inc=f"LA{h % 2}")
            w.op(lambda e, h=h, buf=buf: e.dma_start(gAB[buf][:, :, :],
                                                     g_d[h].rearrange("(t p) f -> p t f", p=128)),
                 inc=f"LA{h % 2}", mark=f"ag_{h}")

    def att_gpsimd(w):
        w.wait(mark=f"fa_umm_{24 * NSC - 1}")
        w.op(lambda e: e.dma_start(wo_sb[:, :, :],
                                   wo_h.ap().rearrange("(kc p) d -> p kc d", p=128)),
             inc="LG4", mark="w_o")

    gsimap = {}
    _g = 0
    for _h in range(H):
        for _G in range(NG):
            for _J in jlists[(_h, _G)]:
                gsimap[(_h, _G, _J)] = _g
                _g += 1
    NSI = _g
    NSEC = H * NG

    def _sec(s_):
        return s_ // NG, s_ % NG

    def _evac_eng(h, G, idx, gsi):
        if idx < 2:
            return "scalar"
        return "scalar" if gsi % 2 == 0 else "vector"

    ALAG = 5

    def att_tensor(w):
        for sec in range(NSEC):
            h, G = _sec(sec)
            buf = h % 2
            qs, ks, v = qsAB[buf], ksAB[buf], vAB[buf]
            if G == 0:
                w.wait(mark=f"ag_{h}")
            jl = jlists[(h, G)]
            for idx, J in enumerate(jl):
                gsi = gsimap[(h, G, J)]
                W0 = 256 if J <= 2 * G else 128
                qoff = G * 256 if J <= 2 * G else G * 256 + 128
                if gsi >= 4:
                    w.wait(mark=f"aev_{gsi-4}")
                w.op(lambda e, gsi=gsi, J=J, qoff=qoff, W0=W0, qs=qs, ks=ks:
                     e.matmul(p.psums[4 + gsi % 4][:, 0:W0],
                              ks[:, J * 128:(J + 1) * 128], qs[:, qoff:qoff + W0],
                              start=True, stop=True),
                     inc="PE", mark=f"as_{gsi}")
                if idx >= ALAG:
                    _att_outmms(w, sec, jl[idx - ALAG], v)
            for J in (jl[-ALAG:] if len(jl) >= ALAG else jl):
                _att_outmms(w, sec, J, v)
            if G == NG - 1:
                w.op(lambda e: e.nop(), inc="PE", mark=f"ahead_{h}")

    def _att_outmms(w, sec, J, v):
        h, G = _sec(sec)
        jl = jlists[(h, G)]
        gsi = gsimap[(h, G, J)]
        w.wait(mark=f"aev_{gsi}")
        subs = [s_ for s_ in range(2) if 2 * G + s_ >= J]
        if sec % NG == 0 and J == jl[0]:
            w.wait(mark=f"vcol_{_sec(sec)[0]}")
        for sub in subs:
            slot = (sec % 2) * 2 + sub
            if J == jl[0] and sec >= 2:
                hp, Gp = _sec(sec - 2)
                w.wait(mark=f"agate_{hp}_{Gp}_{sub}")
                w.wait(mark=f"assq_{hp}_{Gp}_{sub}")
            qoff = G * 256 if J <= 2 * G else G * 256 + 128
            coff = sub * 128 - (qoff - G * 256)
            mks = []
            if J == 2 * G + sub:
                mks.append(f"aostop_{h}_{G}_{sub}")
            if sub == subs[-1]:
                mks.append(f"aout_{gsi}")
            w.op(lambda e, gsi=gsi, sub=sub, slot=slot, coff=coff, J=J, v=v, jl=jl, G=G:
                 e.matmul(p.psums[slot][:, 0:257],
                          _sbf(gsi)[:, coff:coff + 128], v[:, J, :],
                          start=(J == jl[0]), stop=(J == 2 * G + sub)),
                 inc="PE", mark=mks)

    def att_scalar(w):
        asq_prev = [None]
        for sec in range(NSEC + 2):
            if sec < NSEC:
                h, G = _sec(sec)
                jl = jlists[(h, G)]
                for idx, J in enumerate(jl):
                    if J >= 2 * G:
                        continue
                    gsi = gsimap[(h, G, J)]
                    if _evac_eng(h, G, idx, gsi) != "scalar":
                        continue
                    c = _c32(dec, h, 2 * G - J)
                    w.wait(mark=f"as_{gsi}")
                    if gsi >= 8:
                        w.wait(mark=f"aout_{gsi-8}")
                    w.op(lambda e, gsi=gsi, c=c:
                         e.activation(_sbf(gsi)[:, :],
                                      p.psums[4 + gsi % 4][:, 0:256],
                                      AF.Copy, scale=c),
                         inc="ACT", mark=f"aev_{gsi}")
            if 1 <= sec <= NSEC:
                hp, Gp = _sec(sec - 1)
                col = ((sec - 1) % 2) * 2
                w.wait(mark=f"am2_{hp}_{Gp}")
                n = w.op(lambda e, col=col:
                         e.activation(le_sb[:, col:col + 2], m2e_sb[:, col:col + 2],
                                      AF.Ln, bias=eps_sb[:, 0:1]), inc="ACT")
                w.wait(sem="ACT", val=n)
                w.op(lambda e, col=col:
                     e.activation(rse_sb[:, col:col + 2], le_sb[:, col:col + 2],
                                  AF.Exp, scale=-0.5),
                     inc="ACT", mark=f"arstd_{hp}_{Gp}")
            if sec < NSEC:
                for sub in range(2):
                    slot = (sec % 2) * 2 + sub
                    w.wait(mark=f"aostop_{h}_{G}_{sub}")
                    w.op(lambda e, slot=slot:
                         e.activation(t0e_sb[:, slot:slot + 1],
                                      p.psums[slot][:, 256:257], AF.Abs), inc="ACT")
                    if asq_prev[0] is not None:
                        w.wait(sem="ACT", val=asq_prev[0])
                    asq_prev[0] = w.op(lambda e, slot=slot:
                         e.activation(junk_sb[:, :], p.psums[slot][:, 0:256], AF.Square,
                                      accum_out=ssqe_sb[:, slot:slot + 1]),
                         inc="ACT", mark=f"assq_{h}_{G}_{sub}")
            if sec >= NG + 1 and (sec - 1) % NG == 0:
                hs = (sec - 1) // NG - 1
                w.wait(mark=f"agate_{hs}_15_1")
                for sub in range(2):
                    w.op(lambda e, hs=hs, sub=sub:
                         e.dma_start(gated_d[hs, sub].rearrange("(t p) f -> p t f", p=128),
                                     gated_sb[:, hs % 2, :, sub * 128:(sub + 1) * 128]),
                         inc=f"SG{hs % 2}", mark=f"gst_{hs}" if sub == 1 else None)

    def _att_dve_evacs(w, sec):
        h, G = _sec(sec)
        jl = jlists[(h, G)]
        for idx, J in enumerate(jl):
            gsi = gsimap[(h, G, J)]
            diag = J >= 2 * G
            if not diag and _evac_eng(h, G, idx, gsi) != "vector":
                continue
            W0 = 256 if J <= 2 * G else 128
            c = _c32(dec, h, 2 * G - J)
            w.wait(mark=f"as_{gsi}")
            if gsi >= 8:
                w.wait(mark=f"aout_{gsi-8}")
            if diag:
                w.op(lambda e, gsi=gsi, W0=W0, c=c:
                     e.scalar_tensor_tensor(_sbf(gsi)[:, 0:W0],
                                            p.psums[4 + gsi % 4][:, 0:W0], c,
                                            tri2_sb[:, 0:W0], ALU.mult, ALU.mult),
                     inc="DVE", mark=f"aev_{gsi}")
            else:
                w.op(lambda e, gsi=gsi, c=c:
                     e.tensor_scalar_mul(_sbf(gsi)[:, :],
                                         p.psums[4 + gsi % 4][:, 0:256], c),
                     inc="DVE", mark=f"aev_{gsi}")

    def att_vector(w):
        w.wait(mark="c_all")
        for sec in range(NSEC + 2):
            if sec < NSEC and sec % NG == 0:
                h = sec // NG
                buf = h % 2
                v = vAB[buf]
                w.wait(mark=f"ag_{h}")
                if h == 0:
                    w.wait(mark="v_ones")
                if h >= 2:
                    w.wait(mark=f"ahead_{h-2}")
                w.op(lambda e, h=h, v=v:
                     e.tensor_scalar_mul(v[:, :, 256], ones_sb[:, :], bt_sb[:, h:h + 1]),
                     inc="DVE", mark=f"vcol_{h}")
                _att_dve_evacs(w, sec)
            if sec >= 1 and sec - 1 < NSEC:
                hp, Gp = _sec(sec - 1)
                col = ((sec - 1) % 2) * 2
                slot = ((sec - 1) % 2) * 2
                I = 2 * Gp
                w.wait(mark=f"assq_{hp}_{Gp}_0")
                w.wait(mark=f"assq_{hp}_{Gp}_1")
                n = w.op(lambda e, col=col, slot=slot, hp=hp:
                         e.tensor_mul(bete_sb[:, col:col + 2], t0e_sb[:, slot:slot + 2],
                                      a2_sb[:, hp * 2:hp * 2 + 2]), inc="DVE")
                w.wait(sem="DVE", val=n)
                n = w.op(lambda e, col=col, hp=hp, I=I:
                         e.tensor_max(bete_sb[:, col:col + 2], bete_sb[:, col:col + 2],
                                      nrm_sb[:, hp * 32 + I:hp * 32 + I + 2]), inc="DVE")
                w.wait(sem="DVE", val=n)
                n = w.op(lambda e, col=col: e.reciprocal(inv0e_sb[:, col:col + 2],
                                                         bete_sb[:, col:col + 2]),
                         inc="DVE")
                w.wait(sem="DVE", val=n)
                n = w.op(lambda e, col=col, hp=hp:
                         e.tensor_mul(inve_sb[:, col:col + 2], inv0e_sb[:, col:col + 2],
                                      a2_sb[:, hp * 2:hp * 2 + 2]), inc="DVE")
                w.wait(sem="DVE", val=n)
                n = w.op(lambda e, col=col:
                         e.tensor_mul(inv2e_sb[:, col:col + 2], inve_sb[:, col:col + 2],
                                      inve_sb[:, col:col + 2]), inc="DVE")
                w.wait(sem="DVE", val=n)
                w.op(lambda e, col=col, slot=slot:
                     e.scalar_tensor_tensor(m2e_sb[:, col:col + 2],
                                            ssqe_sb[:, slot:slot + 2], 1.0 / 256.0,
                                            inv2e_sb[:, col:col + 2],
                                            ALU.mult, ALU.mult),
                     inc="DVE", mark=f"am2_{hp}_{Gp}")
            if sec < NSEC and sec % NG != 0:
                _att_dve_evacs(w, sec)
            if sec >= 1 and sec - 1 < NSEC:
                hp, Gp = _sec(sec - 1)
                g = gAB[hp % 2]
                if Gp == 0 and hp >= 2:
                    w.wait(mark=f"gst_{hp-2}")
                col = ((sec - 1) % 2) * 2
                w.wait(mark=f"arstd_{hp}_{Gp}")
                w.wait(mark=f"am2_{hp}_{Gp}")
                n = w.op(lambda e, col=col:
                         e.tensor_mul(sige_sb[:, col:col + 2], inve_sb[:, col:col + 2],
                                      rse_sb[:, col:col + 2]), inc="DVE")
                w.wait(sem="DVE", val=n)
                for sub in range(2):
                    I = 2 * Gp + sub
                    slot = ((sec - 1) % 2) * 2 + sub
                    w.op(lambda e, col=col, sub=sub, I=I, g=g, hp=hp, slot=slot:
                         e.scalar_tensor_tensor(gated_sb[:, hp % 2, I, :],
                                                p.psums[slot][:, 0:256],
                                                sige_sb[:, col + sub:col + sub + 1], g[:, I, :],
                                                ALU.mult, ALU.mult),
                         inc="DVE", mark=f"agate_{hp}_{Gp}_{sub}")

    def _w_gt_loads(w, kcs, sem, mark):
        w.wait(mark="gst_6")
        w.wait(mark="gst_7")
        for i, kc in enumerate(kcs):
            h, sub = kc // 2, kc % 2
            w.op(lambda e, kc=kc, h=h, sub=sub:
                 e.dma_start_transpose(gT_sb[:, kc, :], gated_d[h, sub]),
                 inc=sem, mark=mark if i == len(kcs) - 1 else None)

    def w_sync(w):
        w.wait(mark="fb_st_62")
        w.wait(mark="fb_st_63")
        _w_gt_loads(w, range(0, 16), "LT", "w_gt_s")
        for t in range(NT):
            if t >= 2:
                w.wait(mark=f"wd_{t-2}_1")
            w.op(lambda e, t=t: e.dma_start(xw_sb[:, t % 2, :], x_h[t * 128:(t + 1) * 128, :]),
                 inc=f"LW{t % 2}")
            w.op(lambda e, t=t: e.dma_start(gf_sb[:, t % 2, :],
                                            gfin_d[t * 128:(t + 1) * 128, :]),
                 inc=f"LW{t % 2}", mark=f"w_ld_{t}")

    def w_tensor(w):
        for mk in ("w_gt_s",):
            w.wait(mark=mk)
        w.wait(mark="w_o")
        for t in range(NT):
            for n2 in range(2):
                wi = t * 2 + n2
                if wi >= 2:
                    w.wait(mark=f"wmul_{(wi - 2) // 2}_{(wi - 2) % 2}")
                _mm_group(w, p.psums[wi % 2][:, 0:512],
                          [(gT_sb[:, kc, t * 128:(t + 1) * 128],
                            wo_sb[:, kc, n2 * 512:(n2 + 1) * 512]) for kc in range(16)],
                          mark=f"w_mm_{t}_{n2}")

    def w_vector(w):
        for t in range(NT):
            w.wait(mark=f"w_ld_{t}")
            if t >= 2:
                w.wait(mark=f"w_yst_{t-2}")
            for n2 in range(2):
                wi = t * 2 + n2
                w.wait(mark=f"w_mm_{t}_{n2}")
                w.op(lambda e, t=t, n2=n2, wi=wi:
                     e.tensor_mul(outw_sb[:, t % 2, n2 * 512:(n2 + 1) * 512],
                                  p.psums[wi % 2][:, 0:512],
                                  gf_sb[:, t % 2, n2 * 512:(n2 + 1) * 512]),
                     inc="DVE", mark=f"wmul_{t}_{n2}")
            for n2 in range(2):
                w.wait(mark=f"wmul_{t}_{n2}")
                w.op(lambda e, t=t, n2=n2:
                     e.tensor_add(outw_sb[:, t % 2, n2 * 512:(n2 + 1) * 512],
                                  outw_sb[:, t % 2, n2 * 512:(n2 + 1) * 512],
                                  xw_sb[:, t % 2, n2 * 512:(n2 + 1) * 512]),
                     inc="DVE", mark=f"wd_{t}_{n2}")

    def w_scalar(w):
        for t in range(NT):
            w.wait(mark=f"wd_{t}_0")
            w.wait(mark=f"wd_{t}_1")
            w.op(lambda e, t=t: e.dma_start(y_h[t * 128:(t + 1) * 128, :], outw_sb[:, t % 2, :]),
                 inc=f"SY{t % 2}", mark=f"w_yst_{t}")
        w.wait(mark=f"w_yst_{NT-2}")
        w.wait(mark=f"w_yst_{NT-1}")

    import contextlib as _ctx

    def with_psums(specs, bodies, name):
        with _ctx.ExitStack() as st:
            p.psums = [st.enter_context(nc.psum_tensor(f"{name}_ps{i}", sh, dt))
                       for i, (sh, dt) in enumerate(specs)]
            _run_stage(p, name, bodies)

    FB = ([128, 512], F32)
    stages = [
        ([([128, 1024], BF16), ([128, 1024], BF16)],
         [("sync", ln_sync), ("gpsimd", ln_gpsimd), ("scalar", ln_scalar),
          ("vector", ln_vector), ("tensor", ln_tensor)], "LN"),
        ([FB] * 4,
         [("sync", pqk_sync), ("tensor", pqk_tensor), ("scalar", pqk_scalar),
          ("vector", pqk_vector)], "PQK"),
        ([FB] * 4,
         [("gpsimd", pvg_gpsimd), ("tensor", pvg_tensor), ("vector", pvg_vector),
          ("scalar", pvg_scalar)], "PVG"),
        ([FB] * 4,
         [("sync", fa_sync), ("tensor", fa_tensor), ("scalar", fa_scalar),
          ("vector", fa_vector)], "FA"),
        ([FB] * 2,
         [("sync", fb_sync), ("tensor", fb_tensor), ("scalar", fb_scalar)], "FB"),
        ([FB] * 8,
         [("sync", att_sync), ("gpsimd", att_gpsimd), ("tensor", att_tensor),
          ("scalar", att_scalar), ("vector", att_vector)], "ATT"),
        ([FB] * 2,
         [("sync", w_sync), ("tensor", w_tensor), ("vector", w_vector),
          ("scalar", w_scalar)], "W"),
    ]
    for specs, bodies, nm in stages[:nstages]:
        with_psums(specs, bodies, nm)

    stack.close()
    return nc


def kernel(L0_attn_x, L0_attn_wq, L0_attn_wk, L0_attn_wv, L0_attn_wg, L0_attn_wo,
           L0_ffn_w_gate, L0_ffn_w_up, L0_ffn_w_down):
    consts, _ = _consts()
    x = np.ascontiguousarray(np.asarray(L0_attn_x, dtype=np.float32))
    shared = dict(
        wq=np.ascontiguousarray(np.asarray(L0_attn_wq, dtype=np.float32)),
        wk=np.ascontiguousarray(np.asarray(L0_attn_wk, dtype=np.float32)),
        wv=np.ascontiguousarray(np.asarray(L0_attn_wv, dtype=np.float32)),
        wga=np.ascontiguousarray(np.asarray(L0_attn_wg, dtype=np.float32)),
        wo=np.ascontiguousarray(np.asarray(L0_attn_wo, dtype=np.float32)),
        fg=np.ascontiguousarray(np.asarray(L0_ffn_w_gate, dtype=np.int32)),
        fu=np.ascontiguousarray(np.asarray(L0_ffn_w_up, dtype=np.int32)),
        fd=np.ascontiguousarray(np.asarray(L0_ffn_w_down, dtype=np.int32)),
        **consts,
    )
    nc = _build()
    in_maps = [dict(x=np.ascontiguousarray(x[c]), **shared) for c in range(8)]
    res = run_bass_kernel_spmd(nc, in_maps, core_ids=list(range(8)))
    return np.stack([np.asarray(r["y"]) for r in res.results], axis=0)


# version 22 (final)
# speedup vs baseline: 1.1762x; 1.1762x over previous
import numpy as np
import concourse.bass as bass
import concourse.mybir as mybir
from concourse.bass_utils import run_bass_kernel_spmd

F32 = mybir.dt.float32
BF16 = mybir.dt.bfloat16
I32 = mybir.dt.int32
AF = mybir.ActivationFunctionType
ALU = mybir.AluOpType
AX = mybir.AxisListType

S, D, H, KD, DV, HID = 4096, 1024, 8, 128, 2048, 3072
NT = S // 128
NSC = S // 512
KC = D // 128
NG = 16
KSCALE = float(KD) ** -0.5

_DEBUG_TAPS = ()

DMA_SEMS = ("LDC", "LX0", "LX1", "LT", "LH0", "LH1", "LA0", "LA1", "LW0", "LW1",
            "LG1", "LG2", "LG3", "LG4", "LG5", "LG6", "FT0", "FT1", "LT2",
            "LTS", "LTV", "FT2", "FT3", "PZ0", "PZ1", "LC0", "LC1", "LC2",
            "SQ0", "SQ1", "SP0", "SP1", "SF0", "SF1", "SG0", "SG1", "SY0", "SY1")


def _consts():
    lin = np.linspace(0.0, 1.0, KD // 2)
    angle = np.repeat(1.0 / 10000.0 ** lin, 2)
    idx = np.arange(S, dtype=np.float64)
    sin = np.sin(idx[:, None] * angle[None, :])
    cos = np.cos(idx[:, None] * angle[None, :])
    cosf = np.ascontiguousarray(cos.T).astype(np.float32)
    sf2 = np.ascontiguousarray(sin.T)
    sf2[0::2, :] *= -1.0
    sinf2 = sf2.astype(np.float32)
    rswap = np.zeros((128, 128), np.float32)
    for t in range(64):
        rswap[2 * t + 1, 2 * t] = 1.0
        rswap[2 * t, 2 * t + 1] = 1.0
    ident = np.eye(128, dtype=np.float32)
    dec = np.log1p(-2.0 ** (-5.0 - np.arange(H, dtype=np.float64)))
    p = np.arange(128, dtype=np.float64)
    a2t = np.zeros((128, 16), np.float32)
    for h in range(H):
        for sub in range(2):
            a2t[:, h * 2 + sub] = np.exp(dec[h] * (p + 128 * sub)).astype(np.float32)
    bt = np.exp(-dec[None, :] * p[:, None]).astype(np.float32)
    nrmt = np.zeros((128, 256), np.float64)
    for h in range(H):
        Dh = np.exp(dec[h])
        for I in range(32):
            i = I * 128 + p
            nrmt[:, h * 32 + I] = np.sqrt((1.0 - Dh ** (i + 1)) / (1.0 - Dh))
    nrmt = nrmt.astype(np.float32)
    jj = np.arange(128)
    tri2 = np.ones((128, 256), np.float32)
    tri2[:, 0:128] = (jj[None, :] >= jj[:, None]).astype(np.float32)
    return dict(cosf=cosf, sinf2=sinf2, rswap=rswap, ident=ident,
                a2t=a2t, bt=bt, nrmt=nrmt, tri2=tri2), dec


def _c32(dec, h, delta):
    return float(np.float32(np.exp(dec[h] * 128.0 * delta) * 2.0 ** -3.5))


class Prog:
    def __init__(self, nc):
        self.nc = nc
        self.C = {k: 0 for k in DMA_SEMS + ("PE", "ACT", "DVE")}
        self.sems = {}
        self.marks = {}
        self.dry = False


class Eng:
    def __init__(self, p, name, eng=None):
        self.p = p
        self.name = name
        self.e = eng

    def op(self, fn, inc=None, mark=None):
        p = self.p
        n = None
        if inc is not None:
            step = 16 if inc in DMA_SEMS else 1
            p.C[inc] += step
            n = p.C[inc]
            if mark is not None:
                for mk in ([mark] if isinstance(mark, str) else mark):
                    if (not p.dry) and mk in p.marks and p.marks[mk] != (inc, n):
                        raise RuntimeError(f"mark drift {mk}: {p.marks[mk]} vs {(inc, n)}")
                    p.marks[mk] = (inc, n)
        if not p.dry:
            ins = fn(self.e)
            if inc is not None:
                ins.then_inc(p.sems[inc], 16 if inc in DMA_SEMS else 1)
        return n

    def wait(self, mark=None, sem=None, val=None):
        p = self.p
        if p.dry:
            return
        if mark is not None:
            sem, val = p.marks[mark]
        self.e.wait_ge(p.sems[sem], val)

    def wait_if(self, mark):
        if self.p.dry:
            return
        if mark in self.p.marks:
            self.wait(mark=mark)


def _run_stage(p, name, bodies):
    snap = dict(p.C)
    p.dry = True
    for en, fn in bodies:
        fn(Eng(p, en))
    p.C.clear()
    p.C.update(snap)
    p.dry = False
    with p.nc.Block(name) as blk:
        for en, fn in bodies:
            getattr(blk, en)(lambda eng, fn=fn, en=en: fn(Eng(p, en, eng)))


def _mm_group(w, out_ap, pairs, mark=None):
    n = len(pairs)
    for i, (l, r) in enumerate(pairs):
        last = i == n - 1
        w.op(lambda e, l=l, r=r, i=i, last=last:
             e.matmul(out_ap, l, r, start=(i == 0), stop=last),
             inc="PE" if last else None, mark=mark if last else None)


def _build(nstages=7):
    nc = bass.Bass("TRN2", target_bir_lowering=False)
    _, dec = _consts()

    def din(name, shape, dt):
        return nc.dram_tensor(name, shape, dt, kind="ExternalInput")

    x_h = din("x", [S, D], F32)
    wq_h = din("wq", [D, D], F32)
    wk_h = din("wk", [D, D], F32)
    wv_h = din("wv", [D, DV], F32)
    wga_h = din("wga", [D, DV], F32)
    wo_h = din("wo", [DV, D], F32)
    fg_h = din("fg", [HID, D], I32)
    fu_h = din("fu", [HID, D], I32)
    fd_h = din("fd", [D, HID], I32)
    cosf_h = din("cosf", [128, S], F32)
    sinf2_h = din("sinf2", [128, S], F32)
    rswap_h = din("rswap", [128, 128], F32)
    ident_h = din("ident", [128, 128], F32)
    a2t_h = din("a2t", [128, 16], F32)
    bt_h = din("bt", [128, 8], F32)
    nrmt_h = din("nrmt", [128, 256], F32)
    tri2_h = din("tri2", [128, 256], F32)
    y_h = nc.dram_tensor("y", [S, D], F32, kind="ExternalOutput")

    def scr(name, shape, dt):
        kind = "ExternalOutput" if name in _DEBUG_TAPS else "Internal"
        return nc.dram_tensor(name, shape, dt, kind=kind)

    qs_d = scr("qs_d", [H, 128, S], BF16)
    ks_d = scr("ks_d", [H, 128, S], BF16)
    v_d = scr("v_d", [H, S, 256], BF16)
    g_d = scr("g_d", [H, S, 256], BF16)
    fgb_d = scr("fgb_d", [HID, D], BF16)
    fub_d = scr("fub_d", [HID, D], BF16)
    fdb_d = scr("fdb_d", [D, HID], BF16)
    hid_d = scr("hid_d", [NSC, 24, 128, 512], BF16)
    gfin_d = scr("gfin_d", [S, D], F32)
    gated_d = scr("gated_d", [H, 2, S, 128], BF16)
    xn_d = scr("xn_d", [S, D], BF16) if "xn_d" in _DEBUG_TAPS else None

    a2_sb = nc.alloc_sbuf_tensor("a2_sb", [128, 16], F32)
    bt_sb = nc.alloc_sbuf_tensor("bt_sb", [128, 8], F32)
    nrm_sb = nc.alloc_sbuf_tensor("nrm_sb", [128, 256], F32)
    tri2_sb = nc.alloc_sbuf_tensor("tri2_sb", [128, 256], F32)
    rm_sb = nc.alloc_sbuf_tensor("rm_sb", [128, 128], BF16)
    id_sb = nc.alloc_sbuf_tensor("id_sb", [128, 128], BF16)
    ones_sb = nc.alloc_sbuf_tensor("ones_sb", [128, 32], BF16)
    eps_sb = nc.alloc_sbuf_tensor("eps_sb", [128, 1], F32)
    sm_sb = nc.alloc_sbuf_tensor("sm_sb", [128, 32], F32)
    sq_sb = nc.alloc_sbuf_tensor("sq_sb", [128, 32], F32)
    m_sb = nc.alloc_sbuf_tensor("m_sb", [128, 32], F32)
    var_sb = nc.alloc_sbuf_tensor("var_sb", [128, 32], F32)
    rstd_sb = nc.alloc_sbuf_tensor("rstd_sb", [128, 32], F32)
    nb_sb = nc.alloc_sbuf_tensor("nb_sb", [128, 32], F32)
    t0e_sb = nc.alloc_sbuf_tensor("t0e_sb", [128, 6], F32)
    ssqe_sb = nc.alloc_sbuf_tensor("ssqe_sb", [128, 6], F32)
    bete_sb = nc.alloc_sbuf_tensor("bete_sb", [128, 6], F32)
    inv0e_sb = nc.alloc_sbuf_tensor("inv0e_sb", [128, 6], F32)
    inve_sb = nc.alloc_sbuf_tensor("inve_sb", [128, 6], F32)
    inv2e_sb = nc.alloc_sbuf_tensor("inv2e_sb", [128, 6], F32)
    m2e_sb = nc.alloc_sbuf_tensor("m2e_sb", [128, 6], F32)
    le_sb = nc.alloc_sbuf_tensor("le_sb", [128, 6], F32)
    rse_sb = nc.alloc_sbuf_tensor("rse_sb", [128, 6], F32)
    sige_sb = nc.alloc_sbuf_tensor("sige_sb", [128, 6], F32)
    junk_sb = nc.alloc_sbuf_tensor("junk_sb", [128, 256], BF16)

    ob = nc.lookup_mloc(junk_sb).addr + 512
    ob = (ob + 63) & ~63
    TOP = 229344

    def at(name, shape, dt, off):
        h = nc.alloc_sbuf_tensor_at(name, shape, dt, offset=ob + off)
        per = int(np.prod(shape[1:])) * mybir.dt.size(dt)
        assert ob + off + per <= TOP, (name, ob + off + per)
        return h

    KB = 1024
    xn_f = at("xn_f", [128, KC, S], BF16, 0)
    xln_sb = at("xln_sb", [128, 2, D], F32, 64 * KB)
    xnt_sb = at("xnt_sb", [128, 2, D], BF16, 72 * KB)
    lnjunk_sb = at("lnjunk_sb", [128, D], F32, 76 * KB)
    cos_sb = at("cos_sb", [128, S], F32, 80 * KB)
    sin_sb = at("sin_sb", [128, S], F32, 96 * KB)
    wqk_sb = at("wqk_sb", [128, 2, KC, D], BF16, 112 * KB)
    qsa_sb = at("qsa_sb", [128, 2, S], BF16, 144 * KB)
    ksa_sb = at("ksa_sb", [128, 2, S], BF16, 160 * KB)
    qt_sb = at("qt_sb", [128, 2, 512], BF16, 176 * KB)
    t1_sb = at("t1_sb", [128, 2, 512], F32, 178 * KB)
    t2_sb = at("t2_sb", [128, 2, 512], F32, 186 * KB)
    wvg_n = at("wvg_n", [128, 4, 2, KC, 512], BF16, 64 * KB)
    vout_sb = at("vout_sb", [128, 2, 512], BF16, 128 * KB)
    gout_sb = at("gout_sb", [128, 2, 512], BF16, 130 * KB)
    wgt_sb = at("wgt_sb", [128, KC, HID], BF16, 64 * KB)
    wut_sb = at("wut_sb", [128, KC, HID], BF16, 112 * KB)
    sg_sb = at("sg_sb", [128, 2, 512], F32, 196 * KB)
    hidt_sb = at("hidt_sb", [128, 2, 512], BF16, 200 * KB)
    hid_sb = at("hid_sb", [128, 2, 24, 512], BF16, 49 * KB)
    gfin_sb = at("gfin_sb", [128, 2, 512], F32, 97 * KB)
    wdta_sb = at("wdta_sb", [128, 18, D], BF16, 160 * KB)
    wdtb_sb = at("wdtb_sb", [128, 6, D], BF16, 101 * KB)
    vA_sb = at("vA_sb", [128, 32, 257], BF16, 0)
    qsA_sb = at("qsA_sb", [128, S], BF16, 17 * KB)
    ksA_sb = at("ksA_sb", [128, S], BF16, 25 * KB)
    gA_sb = at("gA_sb", [128, 32, 256], BF16, 33 * KB)
    vB_sb = at("vB_sb", [128, 32, 257], BF16, 64 * KB)
    qsB_sb = at("qsB_sb", [128, S], BF16, 81 * KB)
    ksB_sb = at("ksB_sb", [128, S], BF16, 89 * KB)
    gB_sb = at("gB_sb", [128, 32, 256], BF16, 97 * KB)
    gated_sb = at("gated_sb", [128, 2, 32, 256], BF16, 113 * KB)
    sbfA_sb = at("sbfA_sb", [128, 4, 256], BF16, 145 * KB)
    sbfB_sb = at("sbfB_sb", [128, 4, 256], BF16, 49 * KB)
    gT_sb = at("gT_sb", [128, 16, S], BF16, 0)
    wo_sb = at("wo_sb", [128, 16, D], BF16, 147 * KB)
    gf_sb = at("gf_sb", [128, 2, D], F32, 179 * KB)
    xw_sb = at("xw_sb", [128, 2, D], F32, 187 * KB)
    outw_sb = at("outw_sb", [128, 2, D], F32, 195 * KB)

    vAB = [vA_sb, vB_sb]
    qsAB = [qsA_sb, qsB_sb]
    ksAB = [ksA_sb, ksB_sb]
    gAB = [gA_sb, gB_sb]

    def _sbf(gsi):
        return (sbfA_sb if gsi % 8 < 4 else sbfB_sb)[:, gsi % 4]

    p = Prog(nc)
    import contextlib
    stack = contextlib.ExitStack()
    for s in DMA_SEMS + ("PE", "ACT", "DVE"):
        p.sems[s] = stack.enter_context(nc.semaphore(s))

    jlists = {}
    for h in range(H):
        for G in range(NG):
            jlists[(h, G)] = [J for J in range(2 * G + 2)
                             if _c32(dec, h, 2 * G - J) != 0.0]

    def ln_sync(w):
        w.op(lambda e: e.dma_start(cos_sb[:, :], cosf_h[:, :]), inc="LDC")
        w.op(lambda e: e.dma_start(sin_sb[:, :], sinf2_h[:, :]), inc="LDC")
        w.op(lambda e: e.dma_start(a2_sb[:, :], a2t_h[:, :]), inc="LDC")
        w.op(lambda e: e.dma_start(bt_sb[:, :], bt_h[:, :]), inc="LDC")
        w.op(lambda e: e.dma_start(nrm_sb[:, :], nrmt_h[:, :]), inc="LDC")
        w.op(lambda e: e.dma_start(tri2_sb[:, :], tri2_h[:, :]), inc="LDC", mark="c_all")
        for t in range(NT):
            if t >= 2:
                w.wait(mark=f"ln_sq_{t-2}")
                w.wait(mark=f"ln_sm_{t-2}")
            w.op(lambda e, t=t: e.dma_start(xln_sb[:, t % 2, :], x_h[t * 128:(t + 1) * 128, :]),
                 inc=f"LX{t % 2}", mark=f"ln_x_{t}")
        for t in range(NT):
            if t == 0:
                w.wait(mark="ln_nb")
                w.wait(mark="ln_exp")
            if t >= 2:
                w.wait(mark=f"ln_ap_{t-2}")
            w.op(lambda e, t=t: e.dma_start(xln_sb[:, t % 2, :], x_h[t * 128:(t + 1) * 128, :]),
                 inc=f"LX{t % 2}", mark=f"ln_x2_{t}")

    def ln_gpsimd(w):
        w.op(lambda e: e.dma_start(rm_sb[:, :], rswap_h[:, :]), inc="LG1")
        w.op(lambda e: e.dma_start(id_sb[:, :], ident_h[:, :]), inc="LG1")
        w.op(lambda e: e.dma_start(wqk_sb[:, 0, :, :],
                                   wq_h.ap().rearrange("(kk p) d -> p kk d", p=128)),
             inc="LG1")
        w.op(lambda e: e.dma_start(wqk_sb[:, 1, :, :],
                                   wk_h.ap().rearrange("(kk p) d -> p kk d", p=128)),
             inc="LG1", mark="w_qk_all")

    def ln_vector(w):
        w.op(lambda e: e.memset(eps_sb[:, :], 1e-5), inc="DVE")
        w.op(lambda e: e.memset(ones_sb[:, :], 1.0), inc="DVE", mark="v_ones")
        for t in range(NT):
            w.wait(mark=f"ln_x_{t}")
            w.op(lambda e, t=t: e.reduce_sum(sm_sb[:, t:t + 1], xln_sb[:, t % 2, :], axis=AX.X),
                 inc="DVE", mark=f"ln_sm_{t}")
        w.wait(mark="ln_sq_31")
        w.wait(mark="ln_sm_31")
        n = w.op(lambda e: e.tensor_scalar_mul(m_sb[:, :], sm_sb[:, :], 1.0 / D), inc="DVE")
        w.wait(sem="DVE", val=n)
        n = w.op(lambda e: e.tensor_scalar_mul(var_sb[:, :], sq_sb[:, :], 1.0 / D), inc="DVE")
        w.wait(sem="DVE", val=n)
        n = w.op(lambda e: e.scalar_tensor_tensor(rstd_sb[:, :], m_sb[:, :], -1.0, m_sb[:, :],
                                                  ALU.mult, ALU.mult), inc="DVE")
        w.wait(sem="DVE", val=n)
        n = w.op(lambda e: e.tensor_add(var_sb[:, :], var_sb[:, :], rstd_sb[:, :]), inc="DVE")
        w.wait(sem="DVE", val=n)
        n = w.op(lambda e: e.tensor_scalar_max(var_sb[:, :], var_sb[:, :], 0.0),
                 inc="DVE", mark="ln_var")
        w.wait(sem="DVE", val=n)
        w.wait(mark="ln_exp")
        w.op(lambda e: e.scalar_tensor_tensor(nb_sb[:, :], m_sb[:, :], -1.0, rstd_sb[:, :],
                                              ALU.mult, ALU.mult), inc="DVE", mark="ln_nb")
        ps = p.psums
        for t in range(NT):
            for k in range(KC):
                idx = t * KC + k
                w.wait(mark=f"ln_tr_{idx}")
                w.op(lambda e, t=t, k=k, idx=idx:
                     e.tensor_copy(xn_f[:, k, t * 128:(t + 1) * 128], ps[idx % 2][:, 0:128]),
                     inc="DVE", mark=f"ln_xf_{idx}")

    def ln_scalar(w):
        prev = [None]
        for t in range(NT):
            w.wait(mark=f"ln_x_{t}")
            if prev[0] is not None:
                w.wait(sem="ACT", val=prev[0])
            prev[0] = w.op(lambda e, t=t: e.activation(lnjunk_sb[:, :], xln_sb[:, t % 2, :],
                                                       AF.Square, accum_out=sq_sb[:, t:t + 1]),
                           inc="ACT", mark=f"ln_sq_{t}")
        w.wait(mark="ln_var")
        n = w.op(lambda e: e.activation(rstd_sb[:, :], var_sb[:, :], AF.Ln, bias=eps_sb[:, 0:1]),
                 inc="ACT")
        w.wait(sem="ACT", val=n)
        w.op(lambda e: e.activation(rstd_sb[:, :], rstd_sb[:, :], AF.Exp, scale=-0.5),
             inc="ACT", mark="ln_exp")
        w.wait(mark="ln_nb")
        w.wait(mark="ln_exp")
        for t in range(NT):
            w.wait(mark=f"ln_x2_{t}")
            if t >= 2:
                w.wait(mark=f"ln_tr_{(t - 2) * KC + KC - 1}")
            w.op(lambda e, t=t: e.activation(xnt_sb[:, t % 2, :], xln_sb[:, t % 2, :], AF.Identity,
                                             scale=rstd_sb[:, t:t + 1], bias=nb_sb[:, t:t + 1]),
                 inc="ACT", mark=f"ln_ap_{t}")

    def ln_tensor(w):
        ps = p.psums
        w.wait(mark="w_qk_all")
        for t in range(NT):
            w.wait(mark=f"ln_ap_{t}")
            for k in range(KC):
                idx = t * KC + k
                if idx >= 2:
                    w.wait(mark=f"ln_xf_{idx-2}")
                w.op(lambda e, t=t, k=k, idx=idx:
                     e.transpose(ps[idx % 2][:, 0:128], xnt_sb[:, t % 2, k * 128:(k + 1) * 128],
                                 id_sb[:, :]),
                     inc="PE", mark=f"ln_tr_{idx}")

    def pqk_tensor(w):
        w.wait(mark="w_qk_all")
        for ti, (wsel, _asm, pref) in enumerate(((0, qsa_sb, "q"), (1, ksa_sb, "k"))):
            for h in range(H):
                for sc in range(NSC):
                    gi = ti * 64 + h * NSC + sc
                    if gi >= 2:
                        w.wait(mark=f"pqk_ev_{gi-2}")
                    _mm_group(w, p.psums[gi % 2][:, 0:512],
                              [(wqk_sb[:, wsel, kk, h * 128:(h + 1) * 128],
                                xn_f[:, kk, sc * 512:(sc + 1) * 512]) for kk in range(KC)],
                              mark=f"pqk_mm_{gi}")
                    if gi >= 1:
                        _pqk_rot(w, gi - 1)
        _pqk_rot(w, 127)

    def _pqk_rot(w, gi):
        if gi >= 2:
            w.wait(mark=f"pqk_t2_{gi-2}")
        w.wait(mark=f"pqk_ev_{gi}")
        w.op(lambda e, gi=gi: e.matmul(p.psums[2 + gi % 2][:, 0:512], rm_sb[:, :],
                                       qt_sb[:, gi % 2, :], start=True, stop=True),
             inc="PE", mark=f"pqk_rot_{gi}")

    def pqk_scalar(w):
        for ti in range(2):
            for h in range(H):
                for sc in range(NSC):
                    gi = ti * 64 + h * NSC + sc
                    w.wait(mark=f"pqk_mm_{gi}")
                    if gi >= 2:
                        w.wait(mark=f"pqk_rot_{gi-2}")
                        w.wait(mark=f"pqk_t1_{gi-2}")
                    w.op(lambda e, gi=gi: e.activation(qt_sb[:, gi % 2, :],
                                                       p.psums[gi % 2][:, 0:512], AF.Copy),
                         inc="ACT", mark=f"pqk_ev_{gi}")

    def pqk_sync(w):
        for ti in range(2):
            asm = qsa_sb if ti == 0 else ksa_sb
            dst = qs_d if ti == 0 else ks_d
            for h in range(H):
                w.wait(mark=f"pqk_add_{ti}_{h}_{NSC-1}")
                w.op(lambda e, h=h, asm=asm, dst=dst:
                     e.dma_start(dst[h, :, :], asm[:, h % 2, :]),
                     inc=f"SQ{h % 2}", mark=f"pqk_st_{ti}_{h}")

    def pqk_vector(w):
        w.wait(mark="c_all")

        def emit_add(gi):
            ti, r = divmod(gi, 64)
            h, sc = divmod(r, NSC)
            asm = qsa_sb if ti == 0 else ksa_sb
            if h >= 2 and sc == 0:
                w.wait(mark=f"pqk_st_{ti}_{h-2}")
            w.wait(mark=f"pqk_t1_{gi}")
            w.wait(mark=f"pqk_t2_{gi}")
            w.op(lambda e, gi=gi, sc=sc, h=h, asm=asm:
                 e.tensor_add(asm[:, h % 2, sc * 512:(sc + 1) * 512],
                              t1_sb[:, gi % 2, :], t2_sb[:, gi % 2, :]),
                 inc="DVE", mark=f"pqk_add_{ti}_{h}_{sc}")

        for ti in range(2):
            for h in range(H):
                for sc in range(NSC):
                    gi = ti * 64 + h * NSC + sc
                    w.wait(mark=f"pqk_ev_{gi}")
                    if gi >= 2:
                        ti2, r2 = divmod(gi - 2, 64)
                        h2, sc2 = divmod(r2, NSC)
                        w.wait(mark=f"pqk_add_{ti2}_{h2}_{sc2}")
                    w.op(lambda e, gi=gi, sc=sc:
                         e.tensor_mul(t1_sb[:, gi % 2, :], qt_sb[:, gi % 2, :],
                                      cos_sb[:, sc * 512:(sc + 1) * 512]),
                         inc="DVE", mark=f"pqk_t1_{gi}")
                    w.wait(mark=f"pqk_rot_{gi}")
                    w.op(lambda e, gi=gi, sc=sc:
                         e.tensor_mul(t2_sb[:, gi % 2, :], p.psums[2 + gi % 2][:, 0:512],
                                      sin_sb[:, sc * 512:(sc + 1) * 512]),
                         inc="DVE", mark=f"pqk_t2_{gi}")
                    if gi >= 1:
                        emit_add(gi - 1)
        emit_add(127)

    def pvg_gpsimd(w):
        nwaits = {0: ("ln_ap_31", "ln_tr_255", "ln_sq_31"),
                  1: ("pqk_t1_127",), 2: ("pqk_t2_127",), 3: ("pqk_mm_127",)}

        def nload(n):
            for mk in nwaits[n]:
                w.wait(mark=mk)
            nsem = ("LG5", "LG6", "PZ0", "PZ1")[n]
            w.op(lambda e, n=n, nsem=nsem: e.dma_start(
                     wvg_n[:, n, 0, :, :],
                     wv_h[:, n * 512:(n + 1) * 512].rearrange("(kk p) f -> p kk f", p=128)),
                 inc=nsem)
            w.op(lambda e, n=n, nsem=nsem: e.dma_start(
                     wvg_n[:, n, 1, :, :],
                     wga_h[:, n * 512:(n + 1) * 512].rearrange("(kk p) f -> p kk f", p=128)),
                 inc=nsem, mark=f"w_vga_{n}")

        nload(0)
        w.op(lambda e: e.dma_start(fgb_d[:, :], fg_h[:, :]), inc="LC0", mark="w_fg")
        for n in range(1, 4):
            nload(n)
        w.op(lambda e: e.dma_start(fub_d[:, :], fu_h[:, :]), inc="LC1", mark="w_fu")
        w.op(lambda e: e.dma_start(fdb_d[:, :], fd_h[:, :]), inc="LC2",
             mark="w_f_all")

    def pvg_tensor(w):
        for n in range(4):
            w.wait(mark=f"w_vga_{n}")
            for t in range(NT):
                vi = n * NT + t
                if vi >= 2:
                    w.wait(mark=f"pvg_vev_{vi-2}")
                _mm_group(w, p.psums[vi % 2][:, 0:512],
                          [(xn_f[:, kk, t * 128:(t + 1) * 128],
                            wvg_n[:, n, 0, kk, :]) for kk in range(KC)],
                          mark=f"pvg_vmm_{vi}")
                if vi >= 2:
                    w.wait(mark=f"pvg_gev_{vi-2}")
                _mm_group(w, p.psums[2 + vi % 2][:, 0:512],
                          [(xn_f[:, kk, t * 128:(t + 1) * 128],
                            wvg_n[:, n, 1, kk, :]) for kk in range(KC)],
                          mark=f"pvg_gmm_{vi}")

    def pvg_vector(w):
        w.wait(mark="c_all")
        for n in range(4):
            for t in range(NT):
                vi = n * NT + t
                w.wait(mark=f"pvg_vmm_{vi}")
                if vi >= 2:
                    w.wait(mark=f"pvg_s_{vi-2}")
                for half in range(2):
                    w.op(lambda e, vi=vi, n=n, half=half:
                         e.tensor_scalar_mul(vout_sb[:, vi % 2, half * 256:(half + 1) * 256],
                                             p.psums[vi % 2][:, half * 256:(half + 1) * 256],
                                             bt_sb[:, 2 * n + half:2 * n + half + 1]),
                         inc="DVE", mark=f"pvg_vev_{vi}" if half == 1 else None)

    def pvg_scalar(w):
        for n in range(4):
            for t in range(NT):
                vi = n * NT + t
                w.wait(mark=f"pvg_gmm_{vi}")
                if vi >= 2:
                    w.wait(mark=f"pvg_s_{vi-2}")
                w.op(lambda e, vi=vi: e.activation(gout_sb[:, vi % 2, :],
                                                   p.psums[2 + vi % 2][:, 0:512], AF.Silu),
                     inc="ACT", mark=f"pvg_gev_{vi}")
                w.wait(mark=f"pvg_vev_{vi}")
                w.wait(mark=f"pvg_gev_{vi}")
                for half in range(2):
                    w.op(lambda e, vi=vi, t=t, n=n, half=half:
                         e.dma_start(v_d[2 * n + half, t * 128:(t + 1) * 128, :],
                                     vout_sb[:, vi % 2, half * 256:(half + 1) * 256]),
                         inc=f"SP{vi % 2}")
                    w.op(lambda e, vi=vi, t=t, n=n, half=half:
                         e.dma_start(g_d[2 * n + half, t * 128:(t + 1) * 128, :],
                                     gout_sb[:, vi % 2, half * 256:(half + 1) * 256]),
                         inc=f"SP{vi % 2}", mark=f"pvg_s_{vi}" if half == 1 else None)

    def fa_sync(w):
        w.wait(mark="w_fg")
        w.wait(mark="pqk_st_1_6")
        w.wait(mark="pqk_st_1_7")
        w.wait(mark="pvg_s_126")
        w.wait(mark="pvg_s_127")
        for g in range(6):
            if g >= 2:
                w.wait(mark=f"fa_umm_{(4 * (g - 2) + 3) * NSC + NSC - 1}")
            for kk in range(KC):
                w.op(lambda e, g=g, kk=kk: e.dma_start_transpose(
                         wgt_sb[:, kk, g * 512:(g + 1) * 512],
                         fgb_d[g * 512:(g + 1) * 512, kk * 128:(kk + 1) * 128]),
                     inc=f"FT{(g % 2) * 2}", mark=f"fa_wg_{g}" if kk == KC - 1 else None)
            if g == 0:
                w.wait(mark="w_fu")
            for kk in range(KC):
                w.op(lambda e, g=g, kk=kk: e.dma_start_transpose(
                         wut_sb[:, kk, g * 512:(g + 1) * 512],
                         fub_d[g * 512:(g + 1) * 512, kk * 128:(kk + 1) * 128]),
                     inc=f"FT{(g % 2) * 2 + 1}", mark=f"fa_wu_{g}" if kk == KC - 1 else None)
        w.wait(mark="w_f_all")
        for hc in range(18):
            w.op(lambda e, hc=hc: e.dma_start_transpose(wdta_sb[:, hc, :],
                                                        fdb_d[:, hc * 128:(hc + 1) * 128]),
                 inc="LT")

    def fa_tensor(w):
        for hc in range(24):
            if hc % 4 == 0:
                w.wait(mark=f"fa_wg_{hc // 4}")
                w.wait(mark=f"fa_wu_{hc // 4}")
            for sc in range(NSC):
                gi = hc * NSC + sc
                if gi >= 2:
                    w.wait(mark=f"fa_sg_{gi-2}")
                _mm_group(w, p.psums[gi % 2][:, 0:512],
                          [(wgt_sb[:, kk, hc * 128:(hc + 1) * 128],
                            xn_f[:, kk, sc * 512:(sc + 1) * 512]) for kk in range(KC)],
                          mark=f"fa_gmm_{gi}")
                if gi >= 2:
                    w.wait(mark=f"fa_hid_{gi-2}")
                _mm_group(w, p.psums[2 + gi % 2][:, 0:512],
                          [(wut_sb[:, kk, hc * 128:(hc + 1) * 128],
                            xn_f[:, kk, sc * 512:(sc + 1) * 512]) for kk in range(KC)],
                          mark=f"fa_umm_{gi}")

    def fa_scalar(w):
        for hc in range(24):
            for sc in range(NSC):
                gi = hc * NSC + sc
                w.wait(mark=f"fa_gmm_{gi}")
                if gi >= 2:
                    w.wait(mark=f"fa_hid_{gi-2}")
                w.op(lambda e, gi=gi: e.activation(sg_sb[:, gi % 2, :],
                                                   p.psums[gi % 2][:, 0:512],
                                                   AF.Silu, scale=1.0 / 127.0),
                     inc="ACT", mark=f"fa_sg_{gi}")
                w.wait(mark=f"fa_hid_{gi}")
                w.op(lambda e, gi=gi, hc=hc, sc=sc:
                     e.dma_start(hid_d[sc, hc, :, :], hidt_sb[:, gi % 2, :]),
                     inc=f"SF{gi % 2}", mark=f"fa_st_{gi}")

    def fa_vector(w):
        for hc in range(24):
            for sc in range(NSC):
                gi = hc * NSC + sc
                w.wait(mark=f"fa_umm_{gi}")
                w.wait(mark=f"fa_sg_{gi}")
                if gi >= 2:
                    w.wait(mark=f"fa_st_{gi-2}")
                w.op(lambda e, gi=gi: e.scalar_tensor_tensor(hidt_sb[:, gi % 2, :],
                                                             p.psums[2 + gi % 2][:, 0:512],
                                                             1.0 / 127.0, sg_sb[:, gi % 2, :],
                                                             ALU.mult, ALU.mult),
                     inc="DVE", mark=f"fa_hid_{gi}")

    def fb_sync(w):
        for hc in range(18, 24):
            w.op(lambda e, hc=hc: e.dma_start_transpose(wdtb_sb[:, hc - 18, :],
                                                        fdb_d[:, hc * 128:(hc + 1) * 128]),
                 inc="LT", mark="fb_w" if hc == 23 else None)
        w.wait(mark=f"fa_st_{24 * NSC - 2}")
        w.wait(mark=f"fa_st_{24 * NSC - 1}")
        for sc in range(NSC):
            if sc >= 2:
                w.wait(mark=f"fb_mm_{(sc - 2) * 8 + 7}")
            w.op(lambda e, sc=sc: e.dma_start(hid_sb[:, sc % 2, :, :],
                                              hid_d[sc].rearrange("hc p f -> p hc f")),
                 inc=f"LH{sc % 2}", mark=f"fb_hid_{sc}")
        w.wait(mark="pqk_st_1_6")
        w.wait(mark="pqk_st_1_7")
        w.wait(mark="pvg_s_126")
        w.wait(mark="pvg_s_127")
        w.op(lambda e: e.dma_start(qsA_sb[:, :], qs_d[0, :, :]), inc="LA0")
        w.op(lambda e: e.dma_start(ksA_sb[:, :], ks_d[0, :, :]), inc="LA0")
        w.op(lambda e: e.dma_start(vA_sb[:, :, 0:256],
                                   v_d[0].rearrange("(t p) f -> p t f", p=128)),
             inc="LA0")
        w.op(lambda e: e.dma_start(gA_sb[:, :, :],
                                   g_d[0].rearrange("(t p) f -> p t f", p=128)),
             inc="LA0", mark="ag_0")

    def fb_tensor(w):
        w.wait(mark="fb_w")
        for sc in range(NSC):
            w.wait(mark=f"fb_hid_{sc}")
            for ss in range(4):
                for n2 in range(2):
                    di = sc * 8 + ss * 2 + n2
                    if di >= 2:
                        w.wait(mark=f"fb_gf_{di-2}")
                    _mm_group(w, p.psums[di % 2][:, 0:512],
                              [(hid_sb[:, sc % 2, hc, ss * 128:(ss + 1) * 128],
                                (wdta_sb[:, hc, n2 * 512:(n2 + 1) * 512] if hc < 18 else
                                 wdtb_sb[:, hc - 18, n2 * 512:(n2 + 1) * 512]))
                               for hc in range(24)],
                              mark=f"fb_mm_{di}")

    def fb_scalar(w):
        for sc in range(NSC):
            for ss in range(4):
                for n2 in range(2):
                    di = sc * 8 + ss * 2 + n2
                    w.wait(mark=f"fb_mm_{di}")
                    if di >= 2:
                        w.wait(mark=f"fb_st_{di-2}")
                    w.op(lambda e, di=di: e.activation(gfin_sb[:, di % 2, :],
                                                       p.psums[di % 2][:, 0:512],
                                                       AF.Silu, scale=1.0 / 127.0),
                         inc="ACT", mark=f"fb_gf_{di}")
                    r0 = sc * 512 + ss * 128
                    w.wait(sem="ACT", val=p.C["ACT"])
                    w.op(lambda e, di=di, r0=r0, n2=n2:
                         e.dma_start(gfin_d[r0:r0 + 128, n2 * 512:(n2 + 1) * 512],
                                     gfin_sb[:, di % 2, :]),
                         inc=f"SF{di % 2}", mark=f"fb_st_{di}")

    def att_sync(w):
        w.wait(mark="fb_st_62")
        w.wait(mark="fb_st_63")
        for h in range(1, H):
            if h >= 2:
                w.wait(mark=f"ahead_{h-2}")
                w.wait(mark=f"agate_{h-2}_15_1")
            buf = h % 2
            w.op(lambda e, h=h, buf=buf: e.dma_start(qsAB[buf][:, :], qs_d[h, :, :]),
                 inc=f"LA{h % 2}")
            w.op(lambda e, h=h, buf=buf: e.dma_start(ksAB[buf][:, :], ks_d[h, :, :]),
                 inc=f"LA{h % 2}")
            w.op(lambda e, h=h, buf=buf: e.dma_start(vAB[buf][:, :, 0:256],
                                                     v_d[h].rearrange("(t p) f -> p t f", p=128)),
                 inc=f"LA{h % 2}")
            w.op(lambda e, h=h, buf=buf: e.dma_start(gAB[buf][:, :, :],
                                                     g_d[h].rearrange("(t p) f -> p t f", p=128)),
                 inc=f"LA{h % 2}", mark=f"ag_{h}")

    def att_gpsimd(w):
        w.wait(mark=f"fa_umm_{24 * NSC - 1}")
        w.op(lambda e: e.dma_start(wo_sb[:, :, :],
                                   wo_h.ap().rearrange("(kc p) d -> p kc d", p=128)),
             inc="LG4", mark="w_o")

    gsimap = {}
    _g = 0
    for _h in range(H):
        for _G in range(NG):
            for _J in jlists[(_h, _G)]:
                gsimap[(_h, _G, _J)] = _g
                _g += 1
    NSI = _g
    NSEC = H * NG

    def _sec(s_):
        return s_ // NG, s_ % NG

    def _evac_eng(h, G, idx, gsi):
        if idx < 2:
            return "scalar"
        return "scalar" if gsi % 2 == 0 else "vector"

    ALAG = 5

    def att_tensor(w):
        for sec in range(NSEC):
            h, G = _sec(sec)
            buf = h % 2
            qs, ks, v = qsAB[buf], ksAB[buf], vAB[buf]
            if G == 0:
                w.wait(mark=f"ag_{h}")
            jl = jlists[(h, G)]
            for idx, J in enumerate(jl):
                gsi = gsimap[(h, G, J)]
                W0 = 256 if J <= 2 * G else 128
                qoff = G * 256 if J <= 2 * G else G * 256 + 128
                if gsi >= 4:
                    w.wait(mark=f"aev_{gsi-4}")
                w.op(lambda e, gsi=gsi, J=J, qoff=qoff, W0=W0, qs=qs, ks=ks:
                     e.matmul(p.psums[4 + gsi % 4][:, 0:W0],
                              ks[:, J * 128:(J + 1) * 128], qs[:, qoff:qoff + W0],
                              start=True, stop=True),
                     inc="PE", mark=f"as_{gsi}")
                if idx >= ALAG:
                    _att_outmms(w, sec, jl[idx - ALAG], v)
            for J in (jl[-ALAG:] if len(jl) >= ALAG else jl):
                _att_outmms(w, sec, J, v)
            if G == NG - 1:
                w.op(lambda e: e.nop(), inc="PE", mark=f"ahead_{h}")

    def _att_outmms(w, sec, J, v):
        h, G = _sec(sec)
        jl = jlists[(h, G)]
        gsi = gsimap[(h, G, J)]
        w.wait(mark=f"aev_{gsi}")
        subs = [s_ for s_ in range(2) if 2 * G + s_ >= J]
        if sec % NG == 0 and J == jl[0]:
            w.wait(mark=f"vcol_{_sec(sec)[0]}")
        for sub in subs:
            slot = (sec % 2) * 2 + sub
            if J == jl[0] and sec >= 2:
                hp, Gp = _sec(sec - 2)
                w.wait(mark=f"agate_{hp}_{Gp}_{sub}")
                w.wait(mark=f"assq_{hp}_{Gp}_{sub}")
            qoff = G * 256 if J <= 2 * G else G * 256 + 128
            coff = sub * 128 - (qoff - G * 256)
            mks = []
            if J == 2 * G + sub:
                mks.append(f"aostop_{h}_{G}_{sub}")
            if sub == subs[-1]:
                mks.append(f"aout_{gsi}")
            w.op(lambda e, gsi=gsi, sub=sub, slot=slot, coff=coff, J=J, v=v, jl=jl, G=G:
                 e.matmul(p.psums[slot][:, 0:257],
                          _sbf(gsi)[:, coff:coff + 128], v[:, J, :],
                          start=(J == jl[0]), stop=(J == 2 * G + sub)),
                 inc="PE", mark=mks)

    def att_scalar(w):
        asq_prev = [None]
        for sec in range(NSEC + 2):
            if sec < NSEC:
                h, G = _sec(sec)
                jl = jlists[(h, G)]
                for idx, J in enumerate(jl):
                    if J >= 2 * G:
                        continue
                    gsi = gsimap[(h, G, J)]
                    if _evac_eng(h, G, idx, gsi) != "scalar":
                        continue
                    c = _c32(dec, h, 2 * G - J)
                    w.wait(mark=f"as_{gsi}")
                    if gsi >= 8:
                        w.wait(mark=f"aout_{gsi-8}")
                    w.op(lambda e, gsi=gsi, c=c:
                         e.activation(_sbf(gsi)[:, :],
                                      p.psums[4 + gsi % 4][:, 0:256],
                                      AF.Copy, scale=c),
                         inc="ACT", mark=f"aev_{gsi}")
            if 1 <= sec <= NSEC:
                hp, Gp = _sec(sec - 1)
                col = ((sec - 1) % 2) * 2
                w.wait(mark=f"am2_{hp}_{Gp}")
                n = w.op(lambda e, col=col:
                         e.activation(le_sb[:, col:col + 2], m2e_sb[:, col:col + 2],
                                      AF.Ln, bias=eps_sb[:, 0:1]), inc="ACT")
                w.wait(sem="ACT", val=n)
                w.op(lambda e, col=col:
                     e.activation(rse_sb[:, col:col + 2], le_sb[:, col:col + 2],
                                  AF.Exp, scale=-0.5),
                     inc="ACT", mark=f"arstd_{hp}_{Gp}")
            if sec < NSEC:
                for sub in range(2):
                    slot = (sec % 2) * 2 + sub
                    w.wait(mark=f"aostop_{h}_{G}_{sub}")
                    w.op(lambda e, slot=slot:
                         e.activation(t0e_sb[:, slot:slot + 1],
                                      p.psums[slot][:, 256:257], AF.Abs), inc="ACT")
                    if asq_prev[0] is not None:
                        w.wait(sem="ACT", val=asq_prev[0])
                    asq_prev[0] = w.op(lambda e, slot=slot:
                         e.activation(junk_sb[:, :], p.psums[slot][:, 0:256], AF.Square,
                                      accum_out=ssqe_sb[:, slot:slot + 1]),
                         inc="ACT", mark=f"assq_{h}_{G}_{sub}")
            if sec >= NG + 1 and (sec - 1) % NG == 0:
                hs = (sec - 1) // NG - 1
                w.wait(mark=f"agate_{hs}_15_1")
                for sub in range(2):
                    w.op(lambda e, hs=hs, sub=sub:
                         e.dma_start(gated_d[hs, sub].rearrange("(t p) f -> p t f", p=128),
                                     gated_sb[:, hs % 2, :, sub * 128:(sub + 1) * 128]),
                         inc=f"SG{hs % 2}", mark=f"gst_{hs}" if sub == 1 else None)

    def _att_dve_evacs(w, sec):
        h, G = _sec(sec)
        jl = jlists[(h, G)]
        for idx, J in enumerate(jl):
            gsi = gsimap[(h, G, J)]
            diag = J >= 2 * G
            if not diag and _evac_eng(h, G, idx, gsi) != "vector":
                continue
            W0 = 256 if J <= 2 * G else 128
            c = _c32(dec, h, 2 * G - J)
            w.wait(mark=f"as_{gsi}")
            if gsi >= 8:
                w.wait(mark=f"aout_{gsi-8}")
            if diag:
                w.op(lambda e, gsi=gsi, W0=W0, c=c:
                     e.scalar_tensor_tensor(_sbf(gsi)[:, 0:W0],
                                            p.psums[4 + gsi % 4][:, 0:W0], c,
                                            tri2_sb[:, 0:W0], ALU.mult, ALU.mult),
                     inc="DVE", mark=f"aev_{gsi}")
            else:
                w.op(lambda e, gsi=gsi, c=c:
                     e.tensor_scalar_mul(_sbf(gsi)[:, :],
                                         p.psums[4 + gsi % 4][:, 0:256], c),
                     inc="DVE", mark=f"aev_{gsi}")

    def att_vector(w):
        w.wait(mark="c_all")
        for sec in range(NSEC + 2):
            if sec < NSEC and sec % NG == 0:
                h = sec // NG
                buf = h % 2
                v = vAB[buf]
                w.wait(mark=f"ag_{h}")
                if h == 0:
                    w.wait(mark="v_ones")
                if h >= 2:
                    w.wait(mark=f"ahead_{h-2}")
                w.op(lambda e, h=h, v=v:
                     e.tensor_scalar_mul(v[:, :, 256], ones_sb[:, :], bt_sb[:, h:h + 1]),
                     inc="DVE", mark=f"vcol_{h}")
                _att_dve_evacs(w, sec)
            if sec >= 1 and sec - 1 < NSEC:
                hp, Gp = _sec(sec - 1)
                col = ((sec - 1) % 2) * 2
                slot = ((sec - 1) % 2) * 2
                I = 2 * Gp
                w.wait(mark=f"assq_{hp}_{Gp}_0")
                w.wait(mark=f"assq_{hp}_{Gp}_1")
                n = w.op(lambda e, col=col, slot=slot, hp=hp:
                         e.tensor_mul(bete_sb[:, col:col + 2], t0e_sb[:, slot:slot + 2],
                                      a2_sb[:, hp * 2:hp * 2 + 2]), inc="DVE")
                w.wait(sem="DVE", val=n)
                n = w.op(lambda e, col=col, hp=hp, I=I:
                         e.tensor_max(bete_sb[:, col:col + 2], bete_sb[:, col:col + 2],
                                      nrm_sb[:, hp * 32 + I:hp * 32 + I + 2]), inc="DVE")
                w.wait(sem="DVE", val=n)
                n = w.op(lambda e, col=col: e.reciprocal(inv0e_sb[:, col:col + 2],
                                                         bete_sb[:, col:col + 2]),
                         inc="DVE")
                w.wait(sem="DVE", val=n)
                n = w.op(lambda e, col=col, hp=hp:
                         e.tensor_mul(inve_sb[:, col:col + 2], inv0e_sb[:, col:col + 2],
                                      a2_sb[:, hp * 2:hp * 2 + 2]), inc="DVE")
                w.wait(sem="DVE", val=n)
                n = w.op(lambda e, col=col:
                         e.tensor_mul(inv2e_sb[:, col:col + 2], inve_sb[:, col:col + 2],
                                      inve_sb[:, col:col + 2]), inc="DVE")
                w.wait(sem="DVE", val=n)
                w.op(lambda e, col=col, slot=slot:
                     e.scalar_tensor_tensor(m2e_sb[:, col:col + 2],
                                            ssqe_sb[:, slot:slot + 2], 1.0 / 256.0,
                                            inv2e_sb[:, col:col + 2],
                                            ALU.mult, ALU.mult),
                     inc="DVE", mark=f"am2_{hp}_{Gp}")
            if sec < NSEC and sec % NG != 0:
                _att_dve_evacs(w, sec)
            if sec >= 1 and sec - 1 < NSEC:
                hp, Gp = _sec(sec - 1)
                g = gAB[hp % 2]
                if Gp == 0 and hp >= 2:
                    w.wait(mark=f"gst_{hp-2}")
                col = ((sec - 1) % 2) * 2
                w.wait(mark=f"arstd_{hp}_{Gp}")
                w.wait(mark=f"am2_{hp}_{Gp}")
                n = w.op(lambda e, col=col:
                         e.tensor_mul(sige_sb[:, col:col + 2], inve_sb[:, col:col + 2],
                                      rse_sb[:, col:col + 2]), inc="DVE")
                w.wait(sem="DVE", val=n)
                for sub in range(2):
                    I = 2 * Gp + sub
                    slot = ((sec - 1) % 2) * 2 + sub
                    w.op(lambda e, col=col, sub=sub, I=I, g=g, hp=hp, slot=slot:
                         e.scalar_tensor_tensor(gated_sb[:, hp % 2, I, :],
                                                p.psums[slot][:, 0:256],
                                                sige_sb[:, col + sub:col + sub + 1], g[:, I, :],
                                                ALU.mult, ALU.mult),
                         inc="DVE", mark=f"agate_{hp}_{Gp}_{sub}")

    def _w_gt_loads(w, kcs, sem, mark):
        w.wait(mark="gst_6")
        w.wait(mark="gst_7")
        for i, kc in enumerate(kcs):
            h, sub = kc // 2, kc % 2
            w.op(lambda e, kc=kc, h=h, sub=sub:
                 e.dma_start_transpose(gT_sb[:, kc, :], gated_d[h, sub]),
                 inc=sem, mark=mark if i == len(kcs) - 1 else None)

    def w_sync(w):
        w.wait(mark="fb_st_62")
        w.wait(mark="fb_st_63")
        _w_gt_loads(w, range(0, 16), "LT", "w_gt_s")
        for t in range(NT):
            if t >= 2:
                w.wait(mark=f"wd_{t-2}_1")
            w.op(lambda e, t=t: e.dma_start(xw_sb[:, t % 2, :], x_h[t * 128:(t + 1) * 128, :]),
                 inc=f"LW{t % 2}")
            w.op(lambda e, t=t: e.dma_start(gf_sb[:, t % 2, :],
                                            gfin_d[t * 128:(t + 1) * 128, :]),
                 inc=f"LW{t % 2}", mark=f"w_ld_{t}")

    def w_tensor(w):
        for mk in ("w_gt_s",):
            w.wait(mark=mk)
        w.wait(mark="w_o")
        for t in range(NT):
            for n2 in range(2):
                wi = t * 2 + n2
                if wi >= 2:
                    w.wait(mark=f"wmul_{(wi - 2) // 2}_{(wi - 2) % 2}")
                _mm_group(w, p.psums[wi % 2][:, 0:512],
                          [(gT_sb[:, kc, t * 128:(t + 1) * 128],
                            wo_sb[:, kc, n2 * 512:(n2 + 1) * 512]) for kc in range(16)],
                          mark=f"w_mm_{t}_{n2}")

    def w_vector(w):
        for t in range(NT):
            w.wait(mark=f"w_ld_{t}")
            if t >= 2:
                w.wait(mark=f"w_yst_{t-2}")
            for n2 in range(2):
                wi = t * 2 + n2
                w.wait(mark=f"w_mm_{t}_{n2}")
                w.op(lambda e, t=t, n2=n2, wi=wi:
                     e.tensor_mul(outw_sb[:, t % 2, n2 * 512:(n2 + 1) * 512],
                                  p.psums[wi % 2][:, 0:512],
                                  gf_sb[:, t % 2, n2 * 512:(n2 + 1) * 512]),
                     inc="DVE", mark=f"wmul_{t}_{n2}")
            for n2 in range(2):
                w.wait(mark=f"wmul_{t}_{n2}")
                w.op(lambda e, t=t, n2=n2:
                     e.tensor_add(outw_sb[:, t % 2, n2 * 512:(n2 + 1) * 512],
                                  outw_sb[:, t % 2, n2 * 512:(n2 + 1) * 512],
                                  xw_sb[:, t % 2, n2 * 512:(n2 + 1) * 512]),
                     inc="DVE", mark=f"wd_{t}_{n2}")

    def w_scalar(w):
        for t in range(NT):
            w.wait(mark=f"wd_{t}_0")
            w.wait(mark=f"wd_{t}_1")
            w.op(lambda e, t=t: e.dma_start(y_h[t * 128:(t + 1) * 128, :], outw_sb[:, t % 2, :]),
                 inc=f"SY{t % 2}", mark=f"w_yst_{t}")
        w.wait(mark=f"w_yst_{NT-2}")
        w.wait(mark=f"w_yst_{NT-1}")

    import contextlib as _ctx

    def with_psums(specs, bodies, name):
        with _ctx.ExitStack() as st:
            p.psums = [st.enter_context(nc.psum_tensor(f"{name}_ps{i}", sh, dt))
                       for i, (sh, dt) in enumerate(specs)]
            _run_stage(p, name, bodies)

    FB = ([128, 512], F32)
    stages = [
        ([([128, 1024], BF16), ([128, 1024], BF16)],
         [("sync", ln_sync), ("gpsimd", ln_gpsimd), ("scalar", ln_scalar),
          ("vector", ln_vector), ("tensor", ln_tensor)], "LN"),
        ([FB] * 4,
         [("sync", pqk_sync), ("tensor", pqk_tensor), ("scalar", pqk_scalar),
          ("vector", pqk_vector)], "PQK"),
        ([FB] * 4,
         [("gpsimd", pvg_gpsimd), ("tensor", pvg_tensor), ("vector", pvg_vector),
          ("scalar", pvg_scalar)], "PVG"),
        ([FB] * 4,
         [("sync", fa_sync), ("tensor", fa_tensor), ("scalar", fa_scalar),
          ("vector", fa_vector)], "FA"),
        ([FB] * 2,
         [("sync", fb_sync), ("tensor", fb_tensor), ("scalar", fb_scalar)], "FB"),
        ([FB] * 8,
         [("sync", att_sync), ("gpsimd", att_gpsimd), ("tensor", att_tensor),
          ("scalar", att_scalar), ("vector", att_vector)], "ATT"),
        ([FB] * 2,
         [("sync", w_sync), ("tensor", w_tensor), ("vector", w_vector),
          ("scalar", w_scalar)], "W"),
    ]
    for specs, bodies, nm in stages[:nstages]:
        with_psums(specs, bodies, nm)

    stack.close()
    return nc


def kernel(L0_attn_x, L0_attn_wq, L0_attn_wk, L0_attn_wv, L0_attn_wg, L0_attn_wo,
           L0_ffn_w_gate, L0_ffn_w_up, L0_ffn_w_down):
    consts, _ = _consts()
    x = np.ascontiguousarray(np.asarray(L0_attn_x, dtype=np.float32))
    shared = dict(
        wq=np.ascontiguousarray(np.asarray(L0_attn_wq, dtype=np.float32)),
        wk=np.ascontiguousarray(np.asarray(L0_attn_wk, dtype=np.float32)),
        wv=np.ascontiguousarray(np.asarray(L0_attn_wv, dtype=np.float32)),
        wga=np.ascontiguousarray(np.asarray(L0_attn_wg, dtype=np.float32)),
        wo=np.ascontiguousarray(np.asarray(L0_attn_wo, dtype=np.float32)),
        fg=np.ascontiguousarray(np.asarray(L0_ffn_w_gate, dtype=np.int32)),
        fu=np.ascontiguousarray(np.asarray(L0_ffn_w_up, dtype=np.int32)),
        fd=np.ascontiguousarray(np.asarray(L0_ffn_w_down, dtype=np.int32)),
        **consts,
    )
    nc = _build()
    in_maps = [dict(x=np.ascontiguousarray(x[c]), **shared) for c in range(8)]
    res = run_bass_kernel_spmd(nc, in_maps, core_ids=list(range(8)))
    return np.stack([np.asarray(r["y"]) for r in res.results], axis=0)
